# version 03
# baseline (speedup 1.0000x reference)
import numpy as np
import ml_dtypes
import concourse.bass as bass
import concourse.mybir as mybir
from concourse.bass_utils import run_bass_kernel_spmd
from contextlib import ExitStack

F32 = mybir.dt.float32
BF16 = mybir.dt.bfloat16
FP8 = mybir.dt.float8e4
I32 = mybir.dt.int32
I8 = mybir.dt.int8
AF = mybir.ActivationFunctionType
OP = mybir.AluOpType
AX = mybir.AxisListType
DVE_E = mybir.EngineType.DVE
POOL_E = mybir.EngineType.Pool
SP_E = mybir.EngineType.SP

DIM, HID, STATE, HEADS, HEAD_DIM, CONV, CHUNK = 2048, 4096, 128, 64, 64, 4, 256
CONV_DIM = HID + 2 * STATE
D_IN = 2 * HID + 2 * STATE + HEADS
NOF = 67
INTER = 8192
L1L, NH, DH, DHR = 512, 16, 128, 64
DK = DH + DHR
NHC = NH // 2
B, T = 4, 4096
NT = 2048
NCH = NT // CHUNK
KD = DIM // 128
NXT = 3 + NT
SCALE_DK = 1.0 / float(np.sqrt(DK))
FC = 34
HPD = HEADS * HEAD_DIM


class Prog:
    def __init__(self, nc):
        self.nc = nc
        self.items = []
        self.cnt = {}
        self.sems = {}
        self.vals = {}

    def new_sem(self, stack, name):
        self.sems[name] = stack.enter_context(self.nc.semaphore(name))
        self.cnt[name] = 0

    def emit(self, eng, fn, inc=None, incn=1):
        sem = self.sems[inc] if inc is not None else None
        if inc is not None:
            self.cnt[inc] += incn

        def wrapped(h, fn=fn, sem=sem, incn=incn):
            r = fn(h)
            if sem is not None:
                r.then_inc(sem, incn)
        self.items.append((eng, wrapped))

    def bump(self, eng, semname, n):
        self.cnt[semname] += n
        sem = self.sems[semname]
        self.items.append((eng, lambda h: h.sem_inc(sem, n)))

    def wait(self, eng, semname, val=None):
        if val is None:
            val = self.cnt[semname]
        if val <= 0:
            return
        sem = self.sems[semname]
        self.items.append((eng, lambda h: h.wait_ge(sem, val)))

    def bar(self, eng, sems=("DV", "AC", "MM", "GP", "LD", "LG", "ST")):
        for s in sems:
            self.wait(eng, s)

    def snap(self, semname):
        return self.cnt[semname]

    def load_val(self, eng_name, eng_enum, ap_fn, name, lo, hi):
        def fn(h):
            self.vals[(name, eng_name)] = self.nc.values_load(
                ap_fn(), engines=[eng_enum], min_val=lo, max_val=hi,
                skip_runtime_bounds_check=True)
        self.items.append((eng_name, fn))

    def val(self, name, eng_name):
        return self.vals[(name, eng_name)]


class Ar:
    def __init__(s, n, base=0):
        s.n, s.off = n, base

    def a(s, k):
        o = s.off
        s.off += k
        assert s.off <= s.n, f"arena overflow {s.off} > {s.n}"
        return o


def build(stop_after="all", dbg=()):
    nc = bass.Bass("TRN2", target_bir_lowering=False)
    st = ExitStack()

    def dram(name, shape, dt, kind=None):
        k = kind
        if k is None and name in dbg:
            k = "ExternalOutput"
        if k is None:
            return nc.dram_tensor(name, shape, dt).ap()
        return nc.dram_tensor(name, shape, dt, kind=k).ap()

    xin = dram("xin", [DIM, NXT], F32, "ExternalInput")
    w_inproj = dram("w_inproj", [DIM, D_IN], F32, "ExternalInput")
    conv_w = dram("conv_w", [128, FC * CONV], F32, "ExternalInput")
    dt_bias = dram("dt_bias", [HEADS, 1], F32, "ExternalInput")
    a_log = dram("a_log", [HEADS, 1], F32, "ExternalInput")
    d_exp = dram("d_exp", [128, 32], F32, "ExternalInput")
    norm_w = dram("norm_w", [128, 32], F32, "ExternalInput")
    w_outproj = dram("w_outproj", [HID, DIM], F32, "ExternalInput")
    w_gate = dram("w_gate", [128, (INTER // 128) * KD * 128], I8, "ExternalInput")
    w_up = dram("w_up", [128, (INTER // 128) * KD * 128], I8, "ExternalInput")
    w_down = dram("w_down", [INTER, DIM], I8, "ExternalInput")
    s_gate = dram("s_gate", [128, 64], F32, "ExternalInput")
    s_up = dram("s_up", [128, 64], F32, "ExternalInput")
    s_down = dram("s_down", [128, 16], F32, "ExternalInput")
    w_a = dram("w_a", [DIM, 2 * L1L], F32, "ExternalInput")
    b_a = dram("b_a", [128, 8], F32, "ExternalInput")
    w_b = dram("w_b", [2 * L1L, DHR], F32, "ExternalInput")
    b_b = dram("b_b", [DHR, 1], F32, "ExternalInput")
    w_c = dram("w_c", [2 * L1L, NHC * DHR], F32, "ExternalInput")
    b_c = dram("b_c", [128, 4], F32, "ExternalInput")
    w_uq = dram("w_uq", [L1L, NHC * DH], F32, "ExternalInput")
    w_uk = dram("w_uk", [L1L, NHC * DH], F32, "ExternalInput")
    w_uv = dram("w_uv", [L1L, NHC * DH], F32, "ExternalInput")
    w_z = dram("w_z", [NH * DH, DIM], F32, "ExternalInput")
    b_z = dram("b_z", [128, 16], F32, "ExternalInput")
    w_l1f = dram("w_l1f", [DIM, DIM], BF16, "ExternalInput")
    b_l1f = dram("b_l1f", [128, 16], BF16, "ExternalInput")
    cos_k = dram("cos_k", [DHR, T], F32, "ExternalInput")
    sin_k = dram("sin_k", [DHR, T], F32, "ExternalInput")
    cos_q = dram("cos_q", [NHC * DHR, T], F32, "ExternalInput")
    sin_q = dram("sin_q", [NHC * DHR, T], F32, "ExternalInput")
    psign = dram("psign", [128, 128], F32, "ExternalInput")
    meta = dram("meta", [1, 8], I32, "ExternalInput")
    flags = dram("flags", [1, 8], F32, "ExternalInput")

    out = dram("res", [DIM, NT], F32, "ExternalOutput")

    zst = dram("zst", [HID, NT], F32)
    xbcst = dram("xbcst", [CONV_DIM, NXT], F32)
    yst = dram("yst", [HID, NT], F32)
    cstash = dram("cstash", [STATE, NT], F32)
    lstash = dram("lstash", [STATE, NCH * HPD], F32)
    carry_dbg = dram("carry_dbg", [STATE, HPD], F32)
    x1r = dram("x1r", [DIM, NT], F32)
    nst_hi = dram("nst_hi", [HID, NT], BF16)
    nst_lo = dram("nst_lo", [HID, NT], BF16)
    x2r = dram("x2r", [DIM, NT], F32)
    x2st = dram("x2st", [DIM, NT], F32)
    x3r = dram("x3r", [DIM, NT], F32)
    x3st = dram("x3st", [DIM, NT], F32)
    x4r = dram("x4r", [DIM, NT], F32)
    hst_hi = dram("hst_hi", [2 * L1L, 2 * NT], BF16)
    hst_lo = dram("hst_lo", [2 * L1L, 2 * NT], BF16)
    wz_hi = dram("wz_hi", [NH * DH, DIM], BF16)
    wz_lo = dram("wz_lo", [NH * DH, DIM], BF16)
    aost_hi = dram("aost_hi", [NHC * DH, 2 * NT], BF16)
    aost_lo = dram("aost_lo", [NHC * DH, 2 * NT], BF16)
    ao_hi = dram("ao_hi", [NH * DH, NT], BF16)
    ao_lo = dram("ao_lo", [NH * DH, NT], BF16)
    cg_in = dram("cg_in", [STATE, HPD], F32)
    cg_out = dram("cg_out", [2, STATE, HPD], F32)
    hg_in = dram("hg_in", [2 * 2 * L1L, NT], BF16)
    hg_out = dram("hg_out", [2, 2 * 2 * L1L, NT], BF16)
    ag_in = dram("ag_in", [2 * NHC * DH, NT], BF16)
    ag_out = dram("ag_out", [2, 2 * NHC * DH, NT], BF16)
    rowscr = dram("rowscr", [64, 256], F32)
    dacst = dram("dacst", [NCH * 64, 256], F32)
    xpst = dram("xpst", [128, NCH * 2 * HPD], F32)
    xhst = dram("xhst", [128, NCH * 32 * 256], F32)
    dtkst = dram("dtkst", [128, NCH * 256], F32)
    rowscr2 = dram("rowscr2", [64, 1], F32)
    wip_hi = dram("wip_hi", [128, NOF * KD * 128], BF16)
    wip_lo = dram("wip_lo", [128, NOF * KD * 128], BF16)
    wq8 = dram("wq8", [DIM, DIM], FP8)
    agst = dram("agst", [2 * NHC * DH, NT], BF16)
    mx_in = dram("mx_in", [1, 1], F32)
    mx_out = dram("mx_out", [1, 1], F32)

    def sbuf(name, shape, dt):
        return st.enter_context(nc.sbuf_tensor(name, shape, dt))

    AFt = sbuf("AFA", [128, 27300], F32)
    ABt = sbuf("ABA", [128, 18000], BF16)
    AQt = sbuf("AQA", [128, 8704], FP8)
    CONVW = sbuf("CONVW", [128, FC * CONV], F32)
    DTRAW = sbuf("DTRAW", [64, NT], F32)
    DTB = sbuf("DTB", [64, 1], F32)
    AVEC = sbuf("AVEC", [64, 1], F32)
    DEXPt = sbuf("DEXPt", [128, 32], F32)
    NORMW = sbuf("NORMW", [128, 32], F32)
    SGt = sbuf("SGt", [128, 64], F32)
    SUt = sbuf("SUt", [128, 64], F32)
    SDt = sbuf("SDt", [128, 16], F32)
    BAt = sbuf("BAt", [128, 8], F32)
    BBt = sbuf("BBt", [64, 1], F32)
    BCt = sbuf("BCt", [128, 4], F32)
    BZt = sbuf("BZt", [128, 16], F32)
    BL1 = sbuf("BL1", [128, 16], F32)
    BL1B = sbuf("BL1B", [128, 16], BF16)
    METAt = sbuf("METAt", [1, 8], I32)
    FLAGSt = sbuf("FLAGSt", [1, 8], F32)
    IDN = sbuf("IDN", [128, 128], F32)
    ONESt = sbuf("ONESt", [128, 520], F32)
    TRI = sbuf("TRI", [128, 2 * CHUNK], F32)
    AMASK = sbuf("AMASK", [128, 4 * 512], F32)
    ROWB = sbuf("ROWB", [1, 2048], F32)
    PCHUNK = sbuf("PCHUNK", [64, 8], F32)
    EXPA = sbuf("EXPA", [128, 512], F32)
    PSIGN = sbuf("PSIGN", [128, 128], F32)
    IOTP = sbuf("IOTP", [128, 512], I32)
    IOTC = sbuf("IOTC", [128, 512], I32)
    IOTPF = sbuf("IOTPF", [128, 512], F32)
    IOTCF = sbuf("IOTCF", [128, 512], F32)
    PS = [st.enter_context(nc.psum_tensor(f"PS{i}", [128, 512], F32)) for i in range(8)]

    p = Prog(nc)
    for s in ("LD", "LG", "MM", "DV", "AC", "GP", "CC", "ST"):
        p.new_sem(st, s)

    def sy(fn):
        p.emit("sync", fn, inc="LD", incn=16)

    def syw(fn):
        p.emit("sync", fn, inc="ST", incn=16)

    def gdma(fn):
        p.emit("gpsimd", fn, inc="LG", incn=16)

    def dv(fn):
        p.emit("vector", fn, inc="DV")

    def ac(fn):
        p.emit("scalar", fn, inc="AC")

    def te(fn):
        p.emit("tensor", fn, inc="MM")

    def gp(fn):
        p.emit("gpsimd", fn, inc="GP")

    sy(lambda h: h.dma_start(out=CONVW[:, :], in_=conv_w))
    sy(lambda h: h.dma_start(out=DTB[:, :], in_=dt_bias))
    sy(lambda h: h.dma_start(out=AVEC[:, :], in_=a_log))
    sy(lambda h: h.dma_start(out=DEXPt[:, :], in_=d_exp))
    sy(lambda h: h.dma_start(out=NORMW[:, :], in_=norm_w))
    sy(lambda h: h.dma_start(out=SGt[:, :], in_=s_gate))
    sy(lambda h: h.dma_start(out=SUt[:, :], in_=s_up))
    sy(lambda h: h.dma_start(out=SDt[:, :], in_=s_down))
    sy(lambda h: h.dma_start(out=BAt[:, :], in_=b_a))
    sy(lambda h: h.dma_start(out=BBt[:, :], in_=b_b))
    sy(lambda h: h.dma_start(out=BCt[:, :], in_=b_c))
    sy(lambda h: h.dma_start(out=BZt[:, :], in_=b_z))
    sy(lambda h: h.dma_start(out=BL1B[:, :], in_=b_l1f))
    sy(lambda h: h.dma_start(out=METAt[:, :], in_=meta))
    sy(lambda h: h.dma_start(out=FLAGSt[:, :], in_=flags))
    sy(lambda h: h.dma_start(out=PSIGN[:, :], in_=psign))
    p.wait("scalar", "LD")
    ac(lambda h: h.activation(out=AVEC[:, :], in_=AVEC[:, :], func=AF.Exp))
    p.wait("vector", "AC")
    p.wait("vector", "LD")
    dv(lambda h: h.tensor_scalar(out=AVEC[:, :], in0=AVEC[:, :], scalar1=-1.0,
                                 scalar2=None, op0=OP.mult))
    dv(lambda h: h.tensor_copy(out=BL1[:, :], in_=BL1B[:, :]))
    p.emit("gpsimd", lambda h: h.iota(out=IOTP[:, :], base=0, pattern=[[0, 512]], channel_multiplier=1))
    p.emit("gpsimd", lambda h: h.iota(out=IOTC[:, :], base=0, pattern=[[1, 512]], channel_multiplier=0))
    p.emit("gpsimd", lambda h: h.tensor_copy(out=IOTPF[:, :], in_=IOTP[:, :]))
    gp(lambda h: h.tensor_copy(out=IOTCF[:, :], in_=IOTC[:, :]))
    p.wait("vector", "GP")
    dv(lambda h: h.tensor_tensor(out=IDN[:, :], in0=IOTPF[:, :128], in1=IOTCF[:, :128], op=OP.is_equal))
    dv(lambda h: h.memset(ONESt[:, :], 1.0))
    dv(lambda h: h.tensor_tensor(out=TRI[:, 0:CHUNK], in0=IOTPF[:, 0:CHUNK],
                                 in1=IOTCF[:, 0:CHUNK], op=OP.is_le))
    dv(lambda h: h.tensor_scalar(out=IOTPF[:, :], in0=IOTPF[:, :], scalar1=128.0,
                                 scalar2=None, op0=OP.add))
    dv(lambda h: h.tensor_tensor(out=TRI[:, CHUNK:2 * CHUNK], in0=IOTPF[:, 0:CHUNK],
                                 in1=IOTCF[:, 0:CHUNK], op=OP.is_le))
    dv(lambda h: h.tensor_scalar(out=IOTPF[:, :], in0=IOTPF[:, :], scalar1=-128.0,
                                 scalar2=None, op0=OP.add))
    for kb in range(4):
        def mk(h, kb=kb):
            h.tensor_scalar(out=IOTPF[:, :], in0=IOTPF[:, :], scalar1=float(kb * 128),
                            scalar2=None, op0=OP.add)
            r = h.tensor_tensor(out=AMASK[:, kb * 512:(kb + 1) * 512], in0=IOTPF[:, :512],
                                in1=IOTCF[:, :512], op=OP.is_le)
            h.tensor_scalar(out=IOTPF[:, :], in0=IOTPF[:, :], scalar1=float(-kb * 128),
                            scalar2=None, op0=OP.add)
            return r
        dv(mk)

    p.load_val("sync", SP_E, lambda: METAt[0:1, 0:1], "tokbase", 0, NT)
    p.load_val("sync", SP_E, lambda: METAt[0:1, 1:2], "pslot", 0, 1)
    p.load_val("sync", SP_E, lambda: METAt[0:1, 2:3], "rowblk", 0, 8)
    p.load_val("sync", SP_E, lambda: METAt[0:1, 3:4], "ptokbase", 0, NT)
    p.load_val("sync", SP_E, lambda: METAt[0:1, 4:5], "prowblk", 0, 8)
    p0_dv = p.snap("DV")

    fa = Ar(27300)
    ba = Ar(18000)
    o_xstage = fa.a(264)
    o_wf = fa.a(KD * 128)
    o_ev = fa.a(2 * 512)
    o_xh = ba.a(KD * 264)
    o_xl = ba.a(KD * 264)
    o_wh = ba.a(2 * KD * 128)
    o_wl = ba.a(2 * KD * 128)

    cranges = [(0, 259)] + [(3 + 256 * c, 256) for c in range(1, NCH)]

    def emit_inproj():
        slot_mm = {0: 0, 1: 0}
        slot_st = {0: 0, 1: 0}
        for c in range(NCH):
            c0, wdt = cranges[c]
            if c == 1:
                p.wait("sync", "LG")
            p.wait("vector", "MM")
            for k in range(KD):
                p.wait("sync", "DV")
                sy(lambda h, k=k, c0=c0, wdt=wdt: h.dma_start(
                    out=AFt[:, o_xstage:o_xstage + wdt],
                    in_=xin[k * 128:(k + 1) * 128, c0:c0 + wdt]))
                p.wait("vector", "LD")
                ho, lo_ = o_xh + k * 264, o_xl + k * 264
                dv(lambda h, ho=ho, wdt=wdt: h.tensor_copy(
                    out=ABt[:, ho:ho + wdt], in_=AFt[:, o_xstage:o_xstage + wdt]))
                dv(lambda h, lo_=lo_, ho=ho, wdt=wdt: h.tensor_tensor(
                    out=ABt[:, lo_:lo_ + wdt], in0=AFt[:, o_xstage:o_xstage + wdt],
                    in1=ABt[:, ho:ho + wdt], op=OP.subtract))
            for of in range(NOF):
                cw = 128 if of < NOF - 1 else 64
                slot = of % 2
                wh = o_wh + slot * KD * 128
                wl = o_wl + slot * KD * 128
                if c == 0:
                    p.wait("sync", "MM", slot_mm[slot])
                    p.wait("sync", "DV")
                    for k in range(KD):
                        sy(lambda h, k=k, of=of, cw=cw: h.dma_start(
                            out=AFt[:, o_wf + k * 128: o_wf + k * 128 + cw],
                            in_=w_inproj[k * 128:(k + 1) * 128, of * 128: of * 128 + cw]))
                    p.wait("vector", "LD")
                    for k in range(KD):
                        dv(lambda h, k=k, cw=cw, wh=wh: h.tensor_copy(
                            out=ABt[:, wh + k * 128: wh + k * 128 + cw],
                            in_=AFt[:, o_wf + k * 128: o_wf + k * 128 + cw]))
                        dv(lambda h, k=k, cw=cw, wh=wh, wl=wl: h.tensor_tensor(
                            out=ABt[:, wl + k * 128: wl + k * 128 + cw],
                            in0=AFt[:, o_wf + k * 128: o_wf + k * 128 + cw],
                            in1=ABt[:, wh + k * 128: wh + k * 128 + cw], op=OP.subtract))
                    p.wait("gpsimd", "DV")
                    gdma(lambda h, of=of, wh=wh: h.dma_start(
                        out=wip_hi[:, of * KD * 128:(of + 1) * KD * 128],
                        in_=ABt[:, wh:wh + KD * 128]))
                    gdma(lambda h, of=of, wl=wl: h.dma_start(
                        out=wip_lo[:, of * KD * 128:(of + 1) * KD * 128],
                        in_=ABt[:, wl:wl + KD * 128]))
                else:
                    p.wait("sync", "MM", slot_mm[slot])
                    sy(lambda h, of=of, wh=wh: h.dma_start(
                        out=ABt[:, wh:wh + KD * 128],
                        in_=wip_hi[:, of * KD * 128:(of + 1) * KD * 128]))
                    sy(lambda h, of=of, wl=wl: h.dma_start(
                        out=ABt[:, wl:wl + KD * 128],
                        in_=wip_lo[:, of * KD * 128:(of + 1) * KD * 128]))
                p.wait("tensor", "DV")
                if c == 0:
                    p.wait("tensor", "LG")
                else:
                    p.wait("tensor", "LD")
                pbank = PS[of % 4]
                for pi, (ws_, xs_) in enumerate((("h", "h"), ("h", "l"), ("l", "h"))):
                    for k in range(KD):
                        wofs = (wh if ws_ == "h" else wl) + k * 128
                        xofs = (o_xh if xs_ == "h" else o_xl) + k * 264
                        first = (pi == 0 and k == 0)
                        last = (pi == 2 and k == KD - 1)
                        te(lambda h, cw=cw, wdt=wdt, wofs=wofs, xofs=xofs, pbank=pbank,
                           first=first, last=last: h.matmul(
                            out=pbank[:cw, :wdt], lhsT=ABt[:, wofs:wofs + cw],
                            rhs=ABt[:, xofs:xofs + wdt], start=first, stop=last))
                slot_mm[slot] = p.snap("MM")
                eo = o_ev + (of % 2) * 512
                p.wait("vector", "MM")
                p.wait("vector", "ST", slot_st[of % 2])
                dv(lambda h, cw=cw, wdt=wdt, eo=eo, pbank=pbank: h.tensor_copy(
                    out=AFt[:cw, eo:eo + wdt], in_=pbank[:cw, :wdt]))
                p.wait("sync", "DV")
                if of < 32:
                    cz0 = 0 if c == 0 else c0 - 3
                    sk = 3 if c == 0 else 0
                    syw(lambda h, of=of, cz0=cz0, sk=sk, wdt=wdt, eo=eo, cw=cw: h.dma_start(
                        out=zst[of * 128: of * 128 + cw, cz0:cz0 + wdt - sk],
                        in_=AFt[:cw, eo + sk:eo + wdt]))
                elif of < 66:
                    f = of - 32
                    syw(lambda h, f=f, c0=c0, wdt=wdt, eo=eo, cw=cw: h.dma_start(
                        out=xbcst[f * 128: f * 128 + cw, c0:c0 + wdt],
                        in_=AFt[:cw, eo:eo + wdt]))
                else:
                    cz0 = 0 if c == 0 else c0 - 3
                    sk = 3 if c == 0 else 0
                    dv(lambda h, cz0=cz0, sk=sk, wdt=wdt, eo=eo: h.tensor_copy(
                        out=DTRAW[:, cz0:cz0 + wdt - sk],
                        in_=AFt[:64, eo + sk:eo + wdt]))
                    p.bump("sync", "ST", 16)
                slot_st[of % 2] = p.snap("ST")

    if stop_after == "p0":
        pass
    else:
        emit_inproj()
    ph1_st = p.snap("ST")
    ph1_dv = p.snap("DV")

    fa2 = Ar(27300)
    o2_xh = fa2.a(32 * 256)
    o2_xp = fa2.a(2 * HPD)
    o2_carry = fa2.a(HPD)
    o2_raw = fa2.a(2 * 264)
    o2_acc = fa2.a(2 * 256)
    o2_bc = fa2.a(2 * 256)
    o2_btok = fa2.a(2 * 128)
    o2_dtl = fa2.a(2304)
    o2_cbtm = fa2.a(2 * 256)
    o2_mrhs = fa2.a(2 * 512)
    o2_ydev = fa2.a(512)
    o2_cdb = fa2.a(64)
    o2_dttok = fa2.a(2 * 128)
    o2_ndtok = fa2.a(2 * 128)
    assert fa2.off <= 27300

    dl_dt = o2_dtl
    dl_dA = o2_dtl + 256
    dl_dAc = o2_dtl + 512
    dl_ndAc = o2_dtl + 768
    dl_dec = o2_dtl + 1024
    dl_pack = o2_dtl + 1280
    dl_sp1 = o2_dtl + 1536
    dl_sp2 = o2_dtl + 1792
    dl_cd = o2_dtl + 2048
    dl_P = o2_dtl + 2064

    rb_cd = 1024

    p.wait("vector", "DV", p0_dv)
    dv(lambda h: h.memset(AFt[:, o2_carry:o2_carry + HPD], 0.0))
    dv(lambda h: h.memset(AFt[:64, dl_P:dl_P + 16], 1.0))
    dv(lambda h: h.memset(PCHUNK[:, 0:1], 1.0))

    def dtland(c):
        cc0 = 256 * c
        p.bar("scalar", ("DV", "MM"))
        ac(lambda h, cc0=cc0: h.activation(out=AFt[:64, dl_sp1:dl_sp1 + 256],
                                           in_=DTRAW[:, cc0:cc0 + 256],
                                           func=AF.Abs, bias=DTB[:, 0:1]))
        ac(lambda h: h.activation(out=AFt[:64, dl_sp1:dl_sp1 + 256],
                                  in_=AFt[:64, dl_sp1:dl_sp1 + 256], func=AF.Exp, scale=-1.0))
        p.wait("vector", "AC")
        dv(lambda h: h.tensor_scalar(out=AFt[:64, dl_sp2:dl_sp2 + 256],
                                     in0=AFt[:64, dl_sp1:dl_sp1 + 256], scalar1=1.0,
                                     scalar2=None, op0=OP.add))
        ac(lambda h: h.activation(out=AFt[:64, dl_dt:dl_dt + 256],
                                  in_=AFt[:64, dl_sp1:dl_sp1 + 256], func=AF.Ln, bias=1.0))
        p.wait("scalar", "DV")
        ac(lambda h: h.activation(out=AFt[:64, dl_sp1:dl_sp1 + 256],
                                  in_=AFt[:64, dl_dt:dl_dt + 256], func=AF.Exp, scale=-1.0))
        p.wait("vector", "AC")
        dv(lambda h: h.tensor_tensor(out=AFt[:64, dl_sp1:dl_sp1 + 256],
                                     in0=AFt[:64, dl_sp1:dl_sp1 + 256],
                                     in1=AFt[:64, dl_sp2:dl_sp2 + 256], op=OP.mult))
        dv(lambda h: h.tensor_scalar(out=AFt[:64, dl_sp1:dl_sp1 + 256],
                                     in0=AFt[:64, dl_sp1:dl_sp1 + 256], scalar1=-1.0,
                                     scalar2=None, op0=OP.add))
        dv(lambda h: h.tensor_tensor(out=AFt[:64, dl_dt:dl_dt + 256],
                                     in0=AFt[:64, dl_dt:dl_dt + 256],
                                     in1=AFt[:64, dl_sp1:dl_sp1 + 256], op=OP.add))
        p.wait("scalar", "DV")
        ac(lambda h, cc0=cc0: h.activation(out=AFt[:64, dl_sp1:dl_sp1 + 256],
                                           in_=DTRAW[:, cc0:cc0 + 256],
                                           func=AF.Relu, bias=DTB[:, 0:1]))
        p.wait("vector", "AC")
        dv(lambda h: h.tensor_tensor(out=AFt[:64, dl_dt:dl_dt + 256],
                                     in0=AFt[:64, dl_dt:dl_dt + 256],
                                     in1=AFt[:64, dl_sp1:dl_sp1 + 256], op=OP.add))
        dv(lambda h: h.tensor_scalar(out=AFt[:64, dl_dA:dl_dA + 256],
                                     in0=AFt[:64, dl_dt:dl_dt + 256], scalar1=AVEC[:, 0:1],
                                     scalar2=None, op0=OP.mult))
        p.wait("tensor", "DV")
        p.wait("tensor", "AC")
        for tch in range(2):
            te(lambda h, tch=tch: h.transpose(
                out=PS[5][:, tch * 64:tch * 64 + 64],
                in_=AFt[:64, dl_dA + tch * 128: dl_dA + tch * 128 + 128],
                identity=IDN[:64, :64]))
        p.wait("vector", "MM")
        dv(lambda h: h.tensor_copy(out=AFt[:, dl_sp1:dl_sp1 + 128], in_=PS[5][:, :128]))
        p.wait("tensor", "DV")
        for sch in range(2):
            te(lambda h, sch=sch: h.matmul(
                out=PS[6][:64, 0:256],
                lhsT=AFt[:, dl_sp1 + sch * 64: dl_sp1 + sch * 64 + 64],
                rhs=TRI[:, sch * 256: sch * 256 + 256],
                start=(sch == 0), stop=(sch == 1)))
        p.wait("vector", "MM")
        dv(lambda h: h.tensor_copy(out=AFt[:64, dl_dAc:dl_dAc + 256], in_=PS[6][:64, :256]))
        dv(lambda h: h.tensor_scalar(out=AFt[:64, dl_ndAc:dl_ndAc + 256],
                                     in0=AFt[:64, dl_dAc:dl_dAc + 256], scalar1=-1.0,
                                     scalar2=None, op0=OP.mult))
        p.wait("scalar", "DV")
        ac(lambda h: h.activation(out=AFt[:64, dl_dec:dl_dec + 256],
                                  in_=AFt[:64, dl_dAc:dl_dAc + 256], func=AF.Exp, scale=-1.0,
                                  bias=AFt[:64, dl_dAc + 255:dl_dAc + 256]))
        ac(lambda h, c=c: h.activation(out=AFt[:64, dl_cd + c:dl_cd + c + 1],
                                       in_=AFt[:64, dl_dAc + 255:dl_dAc + 256], func=AF.Exp))
        if c > 0:
            dv(lambda h, c=c: h.tensor_tensor(
                out=AFt[:64, dl_P + c:dl_P + c + 1], in0=AFt[:64, dl_P + c - 1:dl_P + c],
                in1=AFt[:64, dl_cd + c - 1:dl_cd + c], op=OP.mult))
            dv(lambda h, c=c: h.tensor_tensor(
                out=PCHUNK[:, c:c + 1], in0=AFt[:64, dl_P + c - 1:dl_P + c],
                in1=AFt[:64, dl_cd + c - 1:dl_cd + c], op=OP.mult))
        p.wait("vector", "AC")
        p.wait("tensor", "DV")
        p.wait("tensor", "AC")
        for tch in range(2):
            te(lambda h, tch=tch: h.transpose(
                out=PS[6][:, tch * 64:tch * 64 + 64],
                in_=AFt[:64, dl_dt + tch * 128: dl_dt + tch * 128 + 128],
                identity=IDN[:64, :64]))
            te(lambda h, tch=tch: h.transpose(
                out=PS[6][:, 128 + tch * 64:128 + tch * 64 + 64],
                in_=AFt[:64, dl_dec + tch * 128: dl_dec + tch * 128 + 128],
                identity=IDN[:64, :64]))
            te(lambda h, tch=tch: h.transpose(
                out=PS[7][:, tch * 64:tch * 64 + 64],
                in_=AFt[:64, dl_ndAc + tch * 128: dl_ndAc + tch * 128 + 128],
                identity=IDN[:64, :64]))
        p.wait("vector", "MM")
        for tch in range(2):
            dv(lambda h, tch=tch: h.tensor_copy(
                out=AFt[:, o2_dttok + tch * 128: o2_dttok + tch * 128 + 64],
                in_=PS[6][:, tch * 64:tch * 64 + 64]))
            dv(lambda h, tch=tch: h.tensor_copy(
                out=AFt[:, o2_dttok + tch * 128 + 64: o2_dttok + tch * 128 + 128],
                in_=PS[6][:, 128 + tch * 64:128 + tch * 64 + 64]))
        dv(lambda h: h.tensor_copy(out=AFt[:, o2_ndtok:o2_ndtok + 128], in_=PS[7][:, :128]))
        p.wait("gpsimd", "DV")
        gdma(lambda h: h.dma_start(out=rowscr[:, 0:256], in_=AFt[:64, dl_dAc:dl_dAc + 256]))
        gdma(lambda h, c=c: h.dma_start(out=dacst[c * 64:(c + 1) * 64, :],
                                        in_=AFt[:64, dl_dAc:dl_dAc + 256]))
        gdma(lambda h, c=c: h.dma_start(out=rowscr2[:, 0:1],
                                        in_=AFt[:64, dl_cd + c:dl_cd + c + 1]))
        p.wait("gpsimd", "LG")
        gdma(lambda h: h.dma_start(
            out=ROWB[0:1, 1024:1024 + 64],
            in_=rowscr2[:, 0:1].rearrange("p x -> (p x)").unsqueeze(0)))

    def conv_strip(c, f, slot_guard):
        cc0 = 256 * c
        slot = f % 2
        ro = o2_raw + slot * 264
        aco = o2_acc + slot * 256
        if slot_guard[slot] is not None:
            p.wait("sync", "DV", slot_guard[slot][0])
        sy(lambda h, f=f, cc0=cc0, ro=ro: h.dma_start(
            out=AFt[:, ro:ro + 259], in_=xbcst[f * 128:(f + 1) * 128, cc0:cc0 + 259]))
        p.wait("vector", "LD")
        if slot_guard[slot] is not None:
            p.wait("vector", "AC", slot_guard[slot][1])
        dv(lambda h, f=f, ro=ro, aco=aco: h.tensor_scalar(
            out=AFt[:, aco:aco + 256], in0=AFt[:, ro:ro + 256],
            scalar1=CONVW[:, f * 4:f * 4 + 1], scalar2=None, op0=OP.mult))
        for t in range(1, 4):
            dv(lambda h, f=f, ro=ro, t=t, aco=aco: h.scalar_tensor_tensor(
                out=AFt[:, aco:aco + 256], in0=AFt[:, ro + t:ro + t + 256],
                scalar=CONVW[:, f * 4 + t:f * 4 + t + 1], in1=AFt[:, aco:aco + 256],
                op0=OP.mult, op1=OP.add))
        raw_done = p.snap("DV")
        if f < 32:
            dst = o2_xh + f * 256
        elif f == 32:
            dst = o2_bc
        else:
            dst = o2_bc + 256
        p.wait("scalar", "DV")
        p.wait("scalar", "MM")
        p.wait("scalar", "ST")
        ac(lambda h, dst=dst, aco=aco: h.activation(
            out=AFt[:, dst:dst + 256], in_=AFt[:, aco:aco + 256], func=AF.Silu))
        slot_guard[slot] = (raw_done, p.snap("AC"))

    def emit_ph2():
        p.wait("sync", "ST", ph1_st)
        for c in range(NCH):
            for e_ in ("tensor", "vector", "scalar", "gpsimd", "sync"):
                p.bar(e_)
            dtland(c)
            slot_guard = {0: None, 1: None}
            for f in range(FC):
                for e_ in ("tensor", "vector", "scalar", "sync", "gpsimd"):
                    p.bar(e_)
                conv_strip(c, f, slot_guard)
                if f < 32:
                    p.wait("tensor", "AC")
                    p.wait("tensor", "DV")
                    for tch in range(2):
                        te(lambda h, f=f, tch=tch: h.transpose(
                            out=PS[6 + (f % 2)][:, tch * 128:tch * 128 + 128],
                            in_=AFt[:, o2_xh + f * 256 + tch * 128: o2_xh + f * 256 + tch * 128 + 128],
                            identity=IDN[:, :]))
                    p.wait("scalar", "MM")
                    for tch in range(2):
                        for hh in range(2):
                            ac(lambda h, f=f, tch=tch, hh=hh: h.activation(
                                out=AFt[:, o2_xp + tch * HPD + f * 128 + hh * 64:
                                        o2_xp + tch * HPD + f * 128 + hh * 64 + 64],
                                in_=PS[6 + (f % 2)][:, tch * 128 + hh * 64:tch * 128 + hh * 64 + 64],
                                func=AF.Identity,
                                scale=AFt[:, o2_dttok + tch * 128 + 2 * f + hh:
                                          o2_dttok + tch * 128 + 2 * f + hh + 1]))
            p.wait("sync", "DV")
            p.wait("sync", "AC")
            syw(lambda h, c=c: h.dma_start(out=xpst[:, c * 2 * HPD:(c + 1) * 2 * HPD],
                                           in_=AFt[:, o2_xp:o2_xp + 2 * HPD]))
            syw(lambda h, c=c: h.dma_start(out=dtkst[:, c * 256:(c + 1) * 256],
                                           in_=AFt[:, o2_dttok:o2_dttok + 256]))
            syw(lambda h, c=c: h.dma_start(out=xhst[:, c * 8192:(c + 1) * 8192],
                                           in_=AFt[:, o2_xh:o2_xh + 8192]))
            p.wait("vector", "ST")
            p.wait("tensor", "AC")
            p.wait("tensor", "DV")
            for tch in range(2):
                te(lambda h, tch=tch: h.transpose(
                    out=PS[6][:, tch * 128:tch * 128 + 128],
                    in_=AFt[:, o2_bc + tch * 128:o2_bc + tch * 128 + 128], identity=IDN[:, :]))
            p.wait("vector", "MM")
            dv(lambda h: h.tensor_copy(out=AFt[:, o2_btok:o2_btok + 256], in_=PS[6][:, :256]))
            p.wait("sync", "AC")
            syw(lambda h, c=c: h.dma_start(out=cstash[:, c * 256:(c + 1) * 256],
                                           in_=AFt[:, o2_bc + 256:o2_bc + 512]))
            p.wait("tensor", "DV")
            for sch in range(2):
                te(lambda h, sch=sch: h.matmul(
                    out=PS[5][:, sch * 256:sch * 256 + 256],
                    lhsT=AFt[:, o2_bc + sch * 128:o2_bc + sch * 128 + 128],
                    rhs=AFt[:, o2_bc + 256:o2_bc + 512], start=True, stop=True))
            p.wait("vector", "MM")
            for sch in range(2):
                dv(lambda h, sch=sch: h.tensor_tensor(
                    out=AFt[:, o2_cbtm + sch * 256:o2_cbtm + sch * 256 + 256],
                    in0=PS[5][:, sch * 256:sch * 256 + 256],
                    in1=TRI[:, sch * 256:sch * 256 + 256], op=OP.mult))
            p.wait("tensor", "DV")
            p.wait("tensor", "LG")
            rowslot_mm = {0: 0, 1: 0}
            for hp2 in range(32):
                for e_ in ("tensor", "vector", "scalar", "gpsimd"):
                    p.bar(e_, ("DV", "AC", "MM", "LG"))
                h0 = 2 * hp2
                ybank = PS[hp2 % 2]
                dbank = PS[2 + (hp2 % 2)]
                ebank = PS[4]
                rslot = (hp2 % 2) * 512
                p.wait("gpsimd", "MM", rowslot_mm[hp2 % 2])
                gdma(lambda h, h0=h0, rslot=rslot: h.dma_start(
                    out=ROWB[0:1, rslot:rslot + 512],
                    in_=rowscr[h0:h0 + 2, 0:256].rearrange("p x -> (p x)").unsqueeze(0)))
                p.wait("tensor", "LG")
                for sch in range(2):
                    p.wait("tensor", "AC")
                    te(lambda h, rslot=rslot, dbank=dbank: h.matmul(
                        out=dbank[:, :512], lhsT=ONESt[0:1, 0:128],
                        rhs=ROWB[0:1, rslot:rslot + 512],
                        start=True, stop=True))
                    if sch == 0:
                        p.wait("scalar", "MM")
                        p.wait("scalar", "DV")
                        ac(lambda h, dbank=dbank: h.activation(
                            out=EXPA[:, :], in_=dbank[:, :512], func=AF.Exp))
                        p.wait("vector", "AC")
                    p.wait("vector", "MM")
                    for hh in range(2):
                        nd = o2_ndtok + sch * 64 + h0 + hh
                        dv(lambda h, hh=hh, nd=nd, dbank=dbank: h.tensor_scalar(
                            out=dbank[:, hh * 256:hh * 256 + 256],
                            in0=dbank[:, hh * 256:hh * 256 + 256],
                            scalar1=AFt[:, nd:nd + 1], scalar2=0.0,
                            op0=OP.add, op1=OP.min))
                    p.wait("scalar", "DV")
                    ac(lambda h, dbank=dbank, ebank=ebank: h.activation(
                        out=ebank[:, :512], in_=dbank[:, :512], func=AF.Exp))
                    p.wait("vector", "AC")
                    p.wait("vector", "MM")
                    for hh in range(2):
                        dv(lambda h, sch=sch, hh=hh, ebank=ebank: h.tensor_tensor(
                            out=AFt[:, o2_mrhs + sch * 512 + hh * 256: o2_mrhs + sch * 512 + hh * 256 + 256],
                            in0=ebank[:, hh * 256:hh * 256 + 256],
                            in1=AFt[:, o2_cbtm + sch * 256:o2_cbtm + sch * 256 + 256],
                            op=OP.mult))
                p.wait("tensor", "DV")
                p.wait("tensor", "AC")
                for hh in range(2):
                    hidx = h0 + hh
                    pb = hh * 64
                    for sch in range(2):
                        te(lambda h, hidx=hidx, hh=hh, sch=sch, pb=pb, ybank=ybank: h.matmul(
                            out=ybank[pb:pb + 64, 0:256],
                            lhsT=AFt[:, o2_xp + sch * HPD + hidx * 64: o2_xp + sch * HPD + hidx * 64 + 64],
                            rhs=AFt[:, o2_mrhs + sch * 512 + hh * 256: o2_mrhs + sch * 512 + hh * 256 + 256],
                            start=(sch == 0), stop=(sch == 1)))
                    te(lambda h, hidx=hidx, pb=pb, ybank=ybank: h.matmul(
                        out=ybank[pb:pb + 64, 256:512],
                        lhsT=AFt[:, o2_carry + hidx * 64: o2_carry + hidx * 64 + 64],
                        rhs=AFt[:, o2_bc + 256:o2_bc + 512], start=True, stop=True))
                rowslot_mm[hp2 % 2] = p.snap("MM")
                f = hp2
                p.wait("vector", "MM")
                for hh in range(2):
                    dv(lambda h, hh=hh, ybank=ybank: h.tensor_tensor(
                        out=AFt[hh * 64:hh * 64 + 64, o2_ydev + 256:o2_ydev + 512],
                        in0=ybank[hh * 64:hh * 64 + 64, 256:512],
                        in1=EXPA[hh * 64:hh * 64 + 64, hh * 256:hh * 256 + 256],
                        op=OP.mult))
                dv(lambda h, f=f, ybank=ybank: h.scalar_tensor_tensor(
                    out=AFt[:, o2_ydev:o2_ydev + 256],
                    in0=AFt[:, o2_xh + f * 256:o2_xh + f * 256 + 256],
                    scalar=DEXPt[:, f:f + 1], in1=ybank[:, 0:256], op0=OP.mult, op1=OP.add))
                dv(lambda h: h.tensor_tensor(
                    out=AFt[:, o2_ydev:o2_ydev + 256], in0=AFt[:, o2_ydev:o2_ydev + 256],
                    in1=AFt[:, o2_ydev + 256:o2_ydev + 512], op=OP.add))
                p.wait("sync", "DV")
                syw(lambda h, c=c, f=f: h.dma_start(
                    out=yst[f * 128:(f + 1) * 128, c * 256:(c + 1) * 256],
                    in_=AFt[:, o2_ydev:o2_ydev + 256]))
                p.wait("vector", "ST")
            for e_ in ("tensor", "vector", "scalar", "sync"):
                p.bar(e_, ("DV", "AC", "MM", "ST"))
            p.wait("sync", "DV")
            syw(lambda h, c=c: h.dma_start(out=lstash[:, c * HPD:(c + 1) * HPD],
                                           in_=AFt[:, o2_carry:o2_carry + HPD]))
            carry_st = p.snap("ST")
            p.wait("scalar", "MM")
            p.wait("scalar", "DV")
            p.wait("scalar", "ST")
            for tch in range(2):
                for hd in range(64):
                    ac(lambda h, tch=tch, hd=hd: h.activation(
                        out=AFt[:, o2_xp + tch * HPD + hd * 64: o2_xp + tch * HPD + hd * 64 + 64],
                        in_=AFt[:, o2_xp + tch * HPD + hd * 64: o2_xp + tch * HPD + hd * 64 + 64],
                        func=AF.Identity,
                        scale=AFt[:, o2_dttok + tch * 128 + 64 + hd:
                                  o2_dttok + tch * 128 + 64 + hd + 1]))
            p.wait("tensor", "DV")
            te(lambda h: h.matmul(out=PS[5][:, :64], lhsT=ONESt[0:1, 0:128],
                                  rhs=ROWB[0:1, rb_cd:rb_cd + 64], start=True, stop=True))
            p.wait("vector", "MM")
            dv(lambda h: h.tensor_copy(out=AFt[:, o2_cdb:o2_cdb + 64], in_=PS[5][:, :64]))
            p.wait("vector", "ST", carry_st)
            p.wait("tensor", "AC")
            for hg in range(8):
                for e_ in ("tensor", "vector"):
                    p.bar(e_, ("DV", "MM"))
                sbank = PS[2 + (hg % 2)]
                p.wait("tensor", "DV")
                p.wait("tensor", "AC")
                for sch in range(2):
                    te(lambda h, hg=hg, sch=sch, sbank=sbank: h.matmul(
                        out=sbank[:, :512],
                        lhsT=AFt[:, o2_btok + sch * 128: o2_btok + sch * 128 + 128],
                        rhs=AFt[:, o2_xp + sch * HPD + hg * 512: o2_xp + sch * HPD + hg * 512 + 512],
                        start=(sch == 0), stop=(sch == 1)))
                p.wait("vector", "MM")
                for hd in range(8):
                    dv(lambda h, hg=hg, hd=hd: h.tensor_scalar(
                        out=AFt[:, o2_carry + hg * 512 + hd * 64: o2_carry + hg * 512 + hd * 64 + 64],
                        in0=AFt[:, o2_carry + hg * 512 + hd * 64: o2_carry + hg * 512 + hd * 64 + 64],
                        scalar1=AFt[:, o2_cdb + hg * 8 + hd: o2_cdb + hg * 8 + hd + 1],
                        scalar2=None, op0=OP.mult))
                dv(lambda h, hg=hg, sbank=sbank: h.tensor_tensor(
                    out=AFt[:, o2_carry + hg * 512: o2_carry + hg * 512 + 512],
                    in0=AFt[:, o2_carry + hg * 512: o2_carry + hg * 512 + 512],
                    in1=sbank[:, :512], op=OP.add))

    if stop_after not in ("p0", "ph1"):
        emit_ph2()

    if stop_after == "ph2":
        p.wait("sync", "DV")
        syw(lambda h: h.dma_start(out=carry_dbg[:, :], in_=AFt[:, o2_carry:o2_carry + HPD]))

    def allbar():
        for e_ in ("sync", "vector", "scalar", "tensor", "gpsimd"):
            p.bar(e_)

    def cc_gather(in_ap, out_ap, groups):
        p.wait("gpsimd", "LG")
        p.wait("gpsimd", "ST")
        p.emit("gpsimd", lambda h: h.collective_compute(
            "AllGather", OP.bypass, replica_groups=groups,
            ins=[in_ap], outs=[out_ap]), inc="CC")
        p.wait("gpsimd", "CC")
        p.bump("gpsimd", "GP", 1)
        p.wait("sync", "GP")

    PAIRS = [[0, 1], [2, 3], [4, 5], [6, 7]]

    def rowfetch(src_ap, rslot, width):
        gdma(lambda h: h.dma_start(out=ROWB[0:1, rslot:rslot + width], in_=src_ap))
        p.wait("tensor", "LG")

    def k1_bcast(row_ap, pbank, cols):
        te(lambda h: h.matmul(out=pbank[:, 0:cols], lhsT=ONESt[0:1, 0:128],
                              rhs=row_ap, start=True, stop=True))
        p.wait("vector", "MM")

    def ln_finalize(pb_sx, pb_sx2, n_d, wdt, o_row, o_ab, pbank):
        dv(lambda h: h.tensor_copy(out=AFt[0:1, o_row:o_row + wdt], in_=pb_sx[0:1, 0:wdt]))
        dv(lambda h: h.tensor_copy(out=AFt[0:1, o_row + wdt:o_row + 2 * wdt],
                                   in_=pb_sx2[0:1, 0:wdt]))
        dv(lambda h: h.tensor_scalar(out=AFt[0:1, o_row:o_row + wdt],
                                     in0=AFt[0:1, o_row:o_row + wdt],
                                     scalar1=1.0 / n_d, scalar2=None, op0=OP.mult))
        dv(lambda h: h.tensor_scalar(out=AFt[0:1, o_row + wdt:o_row + 2 * wdt],
                                     in0=AFt[0:1, o_row + wdt:o_row + 2 * wdt],
                                     scalar1=1.0 / n_d, scalar2=None, op0=OP.mult))
        dv(lambda h: h.tensor_tensor(out=AFt[0:1, o_row + 2 * wdt:o_row + 3 * wdt],
                                     in0=AFt[0:1, o_row:o_row + wdt],
                                     in1=AFt[0:1, o_row:o_row + wdt], op=OP.mult))
        dv(lambda h: h.tensor_tensor(out=AFt[0:1, o_row + 2 * wdt:o_row + 3 * wdt],
                                     in0=AFt[0:1, o_row + wdt:o_row + 2 * wdt],
                                     in1=AFt[0:1, o_row + 2 * wdt:o_row + 3 * wdt],
                                     op=OP.subtract))
        dv(lambda h: h.tensor_scalar(out=AFt[0:1, o_row + 2 * wdt:o_row + 3 * wdt],
                                     in0=AFt[0:1, o_row + 2 * wdt:o_row + 3 * wdt],
                                     scalar1=1e-5, scalar2=None, op0=OP.add))
        p.wait("scalar", "DV")
        ac(lambda h: h.activation(out=AFt[0:1, o_row + 3 * wdt:o_row + 4 * wdt],
                                  in_=AFt[0:1, o_row + 2 * wdt:o_row + 3 * wdt], func=AF.Sqrt))
        p.wait("vector", "AC")
        dv(lambda h: h.reciprocal(out=AFt[0:1, o_row + 4 * wdt:o_row + 5 * wdt],
                                  in_=AFt[0:1, o_row + 3 * wdt:o_row + 4 * wdt]))
        dv(lambda h: h.tensor_tensor(out=AFt[0:1, o_row + 3 * wdt:o_row + 4 * wdt],
                                     in0=AFt[0:1, o_row + 4 * wdt:o_row + 5 * wdt],
                                     in1=AFt[0:1, o_row + 4 * wdt:o_row + 5 * wdt], op=OP.mult))
        dv(lambda h: h.tensor_tensor(out=AFt[0:1, o_row + 3 * wdt:o_row + 4 * wdt],
                                     in0=AFt[0:1, o_row + 3 * wdt:o_row + 4 * wdt],
                                     in1=AFt[0:1, o_row + 2 * wdt:o_row + 3 * wdt], op=OP.mult))
        dv(lambda h: h.tensor_scalar(out=AFt[0:1, o_row + 3 * wdt:o_row + 4 * wdt],
                                     in0=AFt[0:1, o_row + 3 * wdt:o_row + 4 * wdt],
                                     scalar1=-0.5, scalar2=1.5, op0=OP.mult, op1=OP.add))
        dv(lambda h: h.tensor_tensor(out=AFt[0:1, o_row + 4 * wdt:o_row + 5 * wdt],
                                     in0=AFt[0:1, o_row + 4 * wdt:o_row + 5 * wdt],
                                     in1=AFt[0:1, o_row + 3 * wdt:o_row + 4 * wdt], op=OP.mult))
        dv(lambda h: h.tensor_tensor(out=AFt[0:1, o_row + 5 * wdt:o_row + 6 * wdt],
                                     in0=AFt[0:1, o_row:o_row + wdt],
                                     in1=AFt[0:1, o_row + 4 * wdt:o_row + 5 * wdt], op=OP.mult))
        dv(lambda h: h.tensor_scalar(out=AFt[0:1, o_row + 5 * wdt:o_row + 6 * wdt],
                                     in0=AFt[0:1, o_row + 5 * wdt:o_row + 6 * wdt],
                                     scalar1=-1.0, scalar2=None, op0=OP.mult))
        p.wait("tensor", "DV")
        k1_bcast(AFt[0:1, o_row + 4 * wdt:o_row + 5 * wdt], pbank, wdt)
        dv(lambda h: h.tensor_copy(out=AFt[:, o_ab:o_ab + wdt], in_=pbank[:, 0:wdt]))
        p.wait("tensor", "DV")
        k1_bcast(AFt[0:1, o_row + 5 * wdt:o_row + 6 * wdt], pbank, wdt)
        dv(lambda h: h.tensor_copy(out=AFt[:, o_ab + wdt:o_ab + 2 * wdt], in_=pbank[:, 0:wdt]))

    if stop_after == "all":
        allbar()
        gdma(lambda h: h.dma_start(out=cg_in, in_=AFt[:, o2_carry:o2_carry + HPD]))
        cc_gather(cg_in, cg_out, PAIRS)
        allbar()

        fa3 = Ar(27300)
        o3_seed = fa3.a(HPD)
        o3_g = fa3.a(8192)
        o3_z = fa3.a(512)
        o3_c = fa3.a(256)
        o3_pbt = fa3.a(64)
        o3_fb = fa3.a(8)
        o3_sq = fa3.a(256)
        o3_rb = fa3.a(256)
        o3_row = fa3.a(1536)
        o3_nf = fa3.a(256)
        o3_x1 = fa3.a(4096)
        o3_ab = fa3.a(512)
        o3_xin = fa3.a(256)
        o3_nrl = fa3.a(4096)
        o3_nrm = o3_nrl
        assert fa3.off <= 27300, fa3.off
        ba3 = Ar(18000)
        b3_wh = ba3.a(4096)
        b3_wl = ba3.a(4096)
        b3_nh = ba3.a(8192)
        b3_st = ba3.a(512)

        NRMH = ABt[:, b3_nh:b3_nh + 8192]
        NRML = AFt[:, o3_nrl:o3_nrl + 4096].bitcast(BF16)

        def ld_seed(h):
            ps = p.val("pslot", "sync")
            return h.dma_start(out=AFt[:, o3_seed:o3_seed + HPD],
                               in_=cg_out[bass.ds(ps, 1)].rearrange("o p n -> (o p) n"))
        sy(ld_seed)
        allbar()
        k1_bcast(FLAGSt[0:1, 0:1], PS[5], 1)
        dv(lambda h: h.tensor_copy(out=AFt[:, o3_fb:o3_fb + 1], in_=PS[5][:, 0:1]))
        allbar()

        for od in range(16):
            allbar()
            for k in range(32):
                sy(lambda h, od=od, k=k: h.dma_start(
                    out=AFt[:, o3_nrm + k * 128: o3_nrm + k * 128 + 128],
                    in_=w_outproj[k * 128:(k + 1) * 128, od * 128:(od + 1) * 128]))
            p.wait("vector", "LD")
            for k in range(32):
                dv(lambda h, k=k: h.tensor_copy(
                    out=ABt[:, b3_wh + k * 128: b3_wh + k * 128 + 128],
                    in_=AFt[:, o3_nrm + k * 128: o3_nrm + k * 128 + 128]))
                dv(lambda h, k=k: h.tensor_tensor(
                    out=ABt[:, b3_wl + k * 128: b3_wl + k * 128 + 128],
                    in0=AFt[:, o3_nrm + k * 128: o3_nrm + k * 128 + 128],
                    in1=ABt[:, b3_wh + k * 128: b3_wh + k * 128 + 128], op=OP.subtract))
            p.wait("gpsimd", "DV")
            gdma(lambda h, od=od: h.dma_start(out=wip_hi[:, od * 4096:(od + 1) * 4096],
                                              in_=ABt[:, b3_wh:b3_wh + 4096]))
            gdma(lambda h, od=od: h.dma_start(out=wip_lo[:, od * 4096:(od + 1) * 4096],
                                              in_=ABt[:, b3_wl:b3_wl + 4096]))
        allbar()

        for c in range(NCH):
            allbar()
            gdma(lambda h, c=c: h.dma_start(out=rowscr2[:, 0:1], in_=PCHUNK[:, c:c + 1]))
            p.wait("gpsimd", "LG")
            rowfetch(rowscr2[:, 0:1].rearrange("p x -> (p x)").unsqueeze(0), 1024, 64)
            allbar()
            k1_bcast(ROWB[0:1, 1024:1024 + 64], PS[5], 64)
            dv(lambda h: h.tensor_copy(out=AFt[:, o3_pbt:o3_pbt + 64], in_=PS[5][:, 0:64]))
            dv(lambda h: h.tensor_scalar(out=AFt[:, o3_pbt:o3_pbt + 64],
                                         in0=AFt[:, o3_pbt:o3_pbt + 64],
                                         scalar1=AFt[:, o3_fb:o3_fb + 1],
                                         scalar2=None, op0=OP.mult))
            p.wait("sync", "DV")
            for f in range(32):
                sy(lambda h, c=c, f=f: h.dma_start(
                    out=AFt[:, o3_g + f * 256: o3_g + f * 256 + 256],
                    in_=yst[f * 128:(f + 1) * 128, c * 256:(c + 1) * 256]))
            sy(lambda h, c=c: h.dma_start(out=AFt[:, o3_c:o3_c + 256],
                                          in_=cstash[:, c * 256:(c + 1) * 256]))
            allbar()
            for hp2 in range(32):
                h0 = 2 * hp2
                rowfetch(dacst[c * 64 + h0: c * 64 + h0 + 2, :]
                         .rearrange("p x -> (p x)").unsqueeze(0), 0, 512)
                allbar()
                te(lambda h: h.matmul(out=PS[2][:, :512], lhsT=ONESt[0:1, 0:128],
                                      rhs=ROWB[0:1, 0:512], start=True, stop=True))
                p.wait("scalar", "MM")
                ac(lambda h: h.activation(out=EXPA[:, :], in_=PS[2][:, :512], func=AF.Exp))
                for hh in range(2):
                    hidx = h0 + hh
                    dv(lambda h, hidx=hidx, hh=hh: h.tensor_scalar(
                        out=AFt[:, o3_z + hh * 64: o3_z + hh * 64 + 64],
                        in0=AFt[:, o3_seed + hidx * 64: o3_seed + hidx * 64 + 64],
                        scalar1=AFt[:, o3_pbt + hidx:o3_pbt + hidx + 1],
                        scalar2=None, op0=OP.mult))
                p.wait("tensor", "AC")
                p.wait("tensor", "DV")
                for hh in range(2):
                    te(lambda h, hh=hh: h.matmul(
                        out=PS[0][hh * 64:hh * 64 + 64, 0:256],
                        lhsT=AFt[:, o3_z + hh * 64: o3_z + hh * 64 + 64],
                        rhs=AFt[:, o3_c:o3_c + 256], start=True, stop=True))
                p.wait("vector", "MM")
                for hh in range(2):
                    dv(lambda h, hh=hh: h.tensor_tensor(
                        out=AFt[hh * 64:hh * 64 + 64, o3_sq:o3_sq + 256],
                        in0=PS[0][hh * 64:hh * 64 + 64, 0:256],
                        in1=EXPA[hh * 64:hh * 64 + 64, hh * 256:hh * 256 + 256], op=OP.mult))
                f = hp2
                dv(lambda h, f=f: h.tensor_tensor(
                    out=AFt[:, o3_g + f * 256: o3_g + f * 256 + 256],
                    in0=AFt[:, o3_g + f * 256: o3_g + f * 256 + 256],
                    in1=AFt[:, o3_sq:o3_sq + 256], op=OP.add))
                allbar()
            for f in range(32):
                allbar()
                sy(lambda h, c=c, f=f: h.dma_start(
                    out=AFt[:, o3_z:o3_z + 256],
                    in_=zst[f * 128:(f + 1) * 128, c * 256:(c + 1) * 256]))
                p.wait("scalar", "LD")
                ac(lambda h: h.activation(out=AFt[:, o3_z + 256:o3_z + 512],
                                          in_=AFt[:, o3_z:o3_z + 256], func=AF.Silu))
                p.wait("vector", "AC")
                dv(lambda h, f=f: h.tensor_tensor(
                    out=AFt[:, o3_g + f * 256: o3_g + f * 256 + 256],
                    in0=AFt[:, o3_g + f * 256: o3_g + f * 256 + 256],
                    in1=AFt[:, o3_z + 256:o3_z + 512], op=OP.mult))
                p.wait("scalar", "DV")
                ac(lambda h, f=f: h.activation(out=AFt[:, o3_sq:o3_sq + 256],
                                               in_=AFt[:, o3_g + f * 256: o3_g + f * 256 + 256],
                                               func=AF.Square))
                p.wait("tensor", "AC")
                p.wait("tensor", "DV")
                te(lambda h, f=f: h.matmul(out=PS[6][0:1, 0:256], lhsT=ONESt[:, 0:1],
                                           rhs=AFt[:, o3_sq:o3_sq + 256],
                                           start=(f == 0), stop=(f == 31)))
            allbar()
            dv(lambda h: h.tensor_copy(out=AFt[0:1, o3_row:o3_row + 256], in_=PS[6][0:1, 0:256]))
            dv(lambda h: h.tensor_scalar(out=AFt[0:1, o3_row:o3_row + 256],
                                         in0=AFt[0:1, o3_row:o3_row + 256],
                                         scalar1=1.0 / HID, scalar2=1e-5,
                                         op0=OP.mult, op1=OP.add))
            p.wait("scalar", "DV")
            ac(lambda h: h.activation(out=AFt[0:1, o3_row + 256:o3_row + 512],
                                      in_=AFt[0:1, o3_row:o3_row + 256], func=AF.Sqrt))
            p.wait("vector", "AC")
            dv(lambda h: h.reciprocal(out=AFt[0:1, o3_row + 512:o3_row + 768],
                                      in_=AFt[0:1, o3_row + 256:o3_row + 512]))
            dv(lambda h: h.tensor_tensor(out=AFt[0:1, o3_row + 256:o3_row + 512],
                                         in0=AFt[0:1, o3_row + 512:o3_row + 768],
                                         in1=AFt[0:1, o3_row + 512:o3_row + 768], op=OP.mult))
            dv(lambda h: h.tensor_tensor(out=AFt[0:1, o3_row + 256:o3_row + 512],
                                         in0=AFt[0:1, o3_row + 256:o3_row + 512],
                                         in1=AFt[0:1, o3_row:o3_row + 256], op=OP.mult))
            dv(lambda h: h.tensor_scalar(out=AFt[0:1, o3_row + 256:o3_row + 512],
                                         in0=AFt[0:1, o3_row + 256:o3_row + 512],
                                         scalar1=-0.5, scalar2=1.5, op0=OP.mult, op1=OP.add))
            dv(lambda h: h.tensor_tensor(out=AFt[0:1, o3_row + 512:o3_row + 768],
                                         in0=AFt[0:1, o3_row + 512:o3_row + 768],
                                         in1=AFt[0:1, o3_row + 256:o3_row + 512], op=OP.mult))
            allbar()
            k1_bcast(AFt[0:1, o3_row + 512:o3_row + 768], PS[5], 256)
            dv(lambda h: h.tensor_copy(out=AFt[:, o3_rb:o3_rb + 256], in_=PS[5][:, 0:256]))
            allbar()
            for f in range(32):
                dv(lambda h, f=f: h.scalar_tensor_tensor(
                    out=AFt[:, o3_nf:o3_nf + 256],
                    in0=AFt[:, o3_g + f * 256: o3_g + f * 256 + 256],
                    scalar=NORMW[:, f:f + 1], in1=AFt[:, o3_rb:o3_rb + 256],
                    op0=OP.mult, op1=OP.mult))
                dv(lambda h, f=f: h.tensor_copy(out=NRMH[:, f * 256:(f + 1) * 256],
                                                in_=AFt[:, o3_nf:o3_nf + 256]))
                dv(lambda h, f=f: h.tensor_tensor(out=NRML[:, f * 256:(f + 1) * 256],
                                                  in0=AFt[:, o3_nf:o3_nf + 256],
                                                  in1=NRMH[:, f * 256:(f + 1) * 256],
                                                  op=OP.subtract))
            allbar()
            for od in range(16):
                sy(lambda h, od=od: h.dma_start(out=ABt[:, b3_wh:b3_wh + 4096],
                                                in_=wip_hi[:, od * 4096:(od + 1) * 4096]))
                sy(lambda h, od=od: h.dma_start(out=ABt[:, b3_wl:b3_wl + 4096],
                                                in_=wip_lo[:, od * 4096:(od + 1) * 4096]))
                sy(lambda h, c=c, od=od: h.dma_start(
                    out=AFt[:, o3_xin:o3_xin + 256],
                    in_=xin[od * 128:(od + 1) * 128, 3 + c * 256:3 + c * 256 + 256]))
                allbar()
                pbk = PS[od % 2]
                for pi, (ws_, xs_) in enumerate((("h", "h"), ("h", "l"), ("l", "h"))):
                    for k in range(32):
                        wof = (b3_wh if ws_ == "h" else b3_wl) + k * 128
                        xv = (NRMH if xs_ == "h" else NRML)
                        te(lambda h, k=k, wof=wof, xv=xv, pbk=pbk,
                           first=(pi == 0 and k == 0), last=(pi == 2 and k == 31):
                           h.matmul(out=pbk[:, 0:256], lhsT=ABt[:, wof:wof + 128],
                                    rhs=xv[:, k * 256:(k + 1) * 256],
                                    start=first, stop=last))
                p.wait("vector", "MM")
                dv(lambda h, od=od, pbk=pbk: h.tensor_tensor(
                    out=AFt[:, o3_x1 + od * 256: o3_x1 + od * 256 + 256],
                    in0=pbk[:, 0:256], in1=AFt[:, o3_xin:o3_xin + 256], op=OP.add))
                p.wait("scalar", "DV")
                ac(lambda h, od=od: h.activation(out=AFt[:, o3_sq:o3_sq + 256],
                                                 in_=AFt[:, o3_x1 + od * 256: o3_x1 + od * 256 + 256],
                                                 func=AF.Square))
                p.wait("tensor", "AC")
                p.wait("tensor", "DV")
                te(lambda h, od=od: h.matmul(out=PS[6][0:1, 0:256], lhsT=ONESt[:, 0:1],
                                             rhs=AFt[:, o3_x1 + od * 256: o3_x1 + od * 256 + 256],
                                             start=(od == 0), stop=(od == 15)))
                te(lambda h, od=od: h.matmul(out=PS[7][0:1, 0:256], lhsT=ONESt[:, 0:1],
                                             rhs=AFt[:, o3_sq:o3_sq + 256],
                                             start=(od == 0), stop=(od == 15)))
                allbar()
            ln_finalize(PS[6], PS[7], DIM, 256, o3_row, o3_ab, PS[5])
            allbar()
            for od in range(16):
                dv(lambda h, od=od: h.tensor_tensor(
                    out=AFt[:, o3_x1 + od * 256: o3_x1 + od * 256 + 256],
                    in0=AFt[:, o3_x1 + od * 256: o3_x1 + od * 256 + 256],
                    in1=AFt[:, o3_ab:o3_ab + 256], op=OP.mult))
                dv(lambda h, od=od: h.tensor_tensor(
                    out=AFt[:, o3_x1 + od * 256: o3_x1 + od * 256 + 256],
                    in0=AFt[:, o3_x1 + od * 256: o3_x1 + od * 256 + 256],
                    in1=AFt[:, o3_ab + 256:o3_ab + 512], op=OP.add))
                p.wait("sync", "DV")
                syw(lambda h, c=c, od=od: h.dma_start(
                    out=x1r[od * 128:(od + 1) * 128, c * 256:(c + 1) * 256],
                    in_=AFt[:, o3_x1 + od * 256: o3_x1 + od * 256 + 256]))
            allbar()

        fa4 = Ar(27300)
        o4_x1 = fa4.a(4096)
        o4_ff = fa4.a(256)
        o4_fuse = fa4.a(16384)
        o4_x2 = fa4.a(4096)
        o4_sq = fa4.a(256)
        o4_row = fa4.a(1536)
        o4_ab = fa4.a(512)
        assert fa4.off <= 27300, fa4.off
        ba4 = Ar(18000)
        b4_x1h = ba4.a(4096)
        b4_x1l = ba4.a(4096)
        b4_wg = ba4.a(2048)
        b4_wu = ba4.a(2048)
        b4_wd = ba4.a(1024)
        FUSH = AFt[:, o4_fuse:o4_fuse + 8192].bitcast(BF16)
        FUSL = AFt[:, o4_fuse + 8192:o4_fuse + 16384].bitcast(BF16)

        for tt in range(NCH):
            allbar()
            for k in range(KD):
                sy(lambda h, tt=tt, k=k: h.dma_start(
                    out=AFt[:, o4_x1 + k * 256: o4_x1 + k * 256 + 256],
                    in_=x1r[k * 128:(k + 1) * 128, tt * 256:(tt + 1) * 256]))
            p.wait("vector", "LD")
            for k in range(KD):
                dv(lambda h, k=k: h.tensor_copy(out=ABt[:, b4_x1h + k * 256: b4_x1h + k * 256 + 256],
                                                in_=AFt[:, o4_x1 + k * 256: o4_x1 + k * 256 + 256]))
                dv(lambda h, k=k: h.tensor_tensor(out=ABt[:, b4_x1l + k * 256: b4_x1l + k * 256 + 256],
                                                  in0=AFt[:, o4_x1 + k * 256: o4_x1 + k * 256 + 256],
                                                  in1=ABt[:, b4_x1h + k * 256: b4_x1h + k * 256 + 256],
                                                  op=OP.subtract))
            allbar()
            for of in range(64):
                gdma(lambda h, of=of: h.dma_start(
                    out=ABt[:, b4_wg:b4_wg + 2048],
                    in_=w_gate[:, of * 2048:(of + 1) * 2048]))
                gdma(lambda h, of=of: h.dma_start(
                    out=ABt[:, b4_wu:b4_wu + 2048],
                    in_=w_up[:, of * 2048:(of + 1) * 2048]))
                allbar()
                for pi, xs_ in enumerate(("h", "l")):
                    for k in range(KD):
                        xof = (b4_x1h if xs_ == "h" else b4_x1l) + k * 256
                        te(lambda h, k=k, xof=xof, first=(pi == 0 and k == 0),
                           last=(pi == 1 and k == KD - 1): h.matmul(
                            out=PS[0][:, 0:256], lhsT=ABt[:, b4_wg + k * 128: b4_wg + k * 128 + 128],
                            rhs=ABt[:, xof:xof + 256], start=first, stop=last))
                for pi, xs_ in enumerate(("h", "l")):
                    for k in range(KD):
                        xof = (b4_x1h if xs_ == "h" else b4_x1l) + k * 256
                        te(lambda h, k=k, xof=xof, first=(pi == 0 and k == 0),
                           last=(pi == 1 and k == KD - 1): h.matmul(
                            out=PS[1][:, 0:256], lhsT=ABt[:, b4_wu + k * 128: b4_wu + k * 128 + 128],
                            rhs=ABt[:, xof:xof + 256], start=first, stop=last))
                p.wait("scalar", "MM")
                ac(lambda h, of=of: h.activation(out=AFt[:, o4_ff:o4_ff + 256],
                                                 in_=PS[0][:, 0:256], func=AF.Silu,
                                                 scale=SGt[:, of:of + 1]))
                p.wait("vector", "AC")
                p.wait("vector", "MM")
                dv(lambda h, of=of: h.scalar_tensor_tensor(
                    out=AFt[:, o4_ff:o4_ff + 256], in0=PS[1][:, 0:256],
                    scalar=SUt[:, of:of + 1], in1=AFt[:, o4_ff:o4_ff + 256],
                    op0=OP.mult, op1=OP.mult))
                dv(lambda h, of=of: h.tensor_copy(out=FUSH[:, of * 256:(of + 1) * 256],
                                                  in_=AFt[:, o4_ff:o4_ff + 256]))
                dv(lambda h, of=of: h.tensor_tensor(out=FUSL[:, of * 256:(of + 1) * 256],
                                                    in0=AFt[:, o4_ff:o4_ff + 256],
                                                    in1=FUSH[:, of * 256:(of + 1) * 256],
                                                    op=OP.subtract))
                allbar()
            for sweep in range(2):
                allbar()
                for fch in range(64):
                    p.wait("gpsimd", "MM")
                    gdma(lambda h, fch=fch, sweep=sweep: h.dma_start(
                        out=ABt[:, b4_wd:b4_wd + 1024],
                        in_=w_down[fch * 128:(fch + 1) * 128,
                                   sweep * 1024:(sweep + 1) * 1024]))
                    allbar()
                    for d8 in range(8):
                        for pi, fv in enumerate((FUSH, FUSL)):
                            te(lambda h, fch=fch, d8=d8, fv=fv,
                               first=(fch == 0 and pi == 0),
                               last=(fch == 63 and pi == 1): h.matmul(
                                out=PS[d8][:, 0:256],
                                lhsT=ABt[:, b4_wd + d8 * 128: b4_wd + d8 * 128 + 128],
                                rhs=fv[:, fch * 256:(fch + 1) * 256],
                                start=first, stop=last))
                p.wait("vector", "MM")
                for d8 in range(8):
                    od = sweep * 8 + d8
                    dv(lambda h, od=od, d8=d8: h.scalar_tensor_tensor(
                        out=AFt[:, o4_x2 + od * 256: o4_x2 + od * 256 + 256],
                        in0=PS[d8][:, 0:256], scalar=SDt[:, od:od + 1],
                        in1=AFt[:, o4_x1 + od * 256: o4_x1 + od * 256 + 256],
                        op0=OP.mult, op1=OP.add))
                allbar()
            for od in range(16):
                p.wait("scalar", "DV")
                ac(lambda h, od=od: h.activation(out=AFt[:, o4_sq:o4_sq + 256],
                                                 in_=AFt[:, o4_x2 + od * 256: o4_x2 + od * 256 + 256],
                                                 func=AF.Square))
                p.wait("tensor", "AC")
                p.wait("tensor", "DV")
                te(lambda h, od=od: h.matmul(out=PS[6][0:1, 0:256], lhsT=ONESt[:, 0:1],
                                             rhs=AFt[:, o4_x2 + od * 256: o4_x2 + od * 256 + 256],
                                             start=(od == 0), stop=(od == 15)))
                te(lambda h, od=od: h.matmul(out=PS[7][0:1, 0:256], lhsT=ONESt[:, 0:1],
                                             rhs=AFt[:, o4_sq:o4_sq + 256],
                                             start=(od == 0), stop=(od == 15)))
                allbar()
            ln_finalize(PS[6], PS[7], DIM, 256, o4_row, o4_ab, PS[5])
            allbar()
            for od in range(16):
                dv(lambda h, od=od: h.tensor_tensor(
                    out=AFt[:, o4_x2 + od * 256: o4_x2 + od * 256 + 256],
                    in0=AFt[:, o4_x2 + od * 256: o4_x2 + od * 256 + 256],
                    in1=AFt[:, o4_ab:o4_ab + 256], op=OP.mult))
                dv(lambda h, od=od: h.tensor_tensor(
                    out=AFt[:, o4_x2 + od * 256: o4_x2 + od * 256 + 256],
                    in0=AFt[:, o4_x2 + od * 256: o4_x2 + od * 256 + 256],
                    in1=AFt[:, o4_ab + 256:o4_ab + 512], op=OP.add))
                p.wait("sync", "DV")
                syw(lambda h, tt=tt, od=od: h.dma_start(
                    out=x2st[od * 128:(od + 1) * 128, tt * 256:(tt + 1) * 256],
                    in_=AFt[:, o4_x2 + od * 256: o4_x2 + od * 256 + 256]))
            allbar()

        fa5 = Ar(27300)
        o5_hf = fa5.a(512)
        o5_kp = fa5.a(1536)
        o5_cb = fa5.a(1024)
        o5_t1 = fa5.a(512)
        o5_rr = fa5.a(512)
        o5_krtf = fa5.a(4096)
        oU = fa5.a(19104)
        o5_x2 = oU
        o5_bc = oU + 8192
        o5_qk = oU
        o5_qrtf = oU + 12288
        o5_row = oU + 16384
        o5_rb = oU + 17984
        o5_ao = oU + 18496
        ba5 = Ar(18000)
        b5_bh = ba5.a(4096)
        b5_bl = ba5.a(4096)
        b5_wh = ba5.a(1536)
        b5_wl = ba5.a(1536)
        b5_ch = ba5.a(512)
        b5_cl = ba5.a(512)
        b5_ah = ba5.a(512)
        b5_al = ba5.a(512)
        b5_on = ba5.a(8)
        ONESB = ABt[:, b5_on:b5_on + 8]

        QKV = AFt[:, o5_qk:o5_qk + 12288].bitcast(BF16)
        KH, KL, QH, QL, VTH, VTL = 0, 4096, 8192, 12288, 16384, 20480
        X2H = AFt[:, o5_bc:o5_bc + 4096].bitcast(BF16)
        X2L = AFt[:, o5_bc + 4096:o5_bc + 8192].bitcast(BF16)

        for of in range(8):
            allbar()
            for k in range(KD):
                sy(lambda h, of=of, k=k: h.dma_start(
                    out=AFt[:, o5_x2 + k * 128: o5_x2 + k * 128 + 128],
                    in_=w_a[k * 128:(k + 1) * 128, of * 128:(of + 1) * 128]))
            p.wait("vector", "LD")
            for k in range(KD):
                dv(lambda h, k=k: h.tensor_copy(
                    out=ABt[:, b5_bh + k * 128: b5_bh + k * 128 + 128],
                    in_=AFt[:, o5_x2 + k * 128: o5_x2 + k * 128 + 128]))
                dv(lambda h, k=k: h.tensor_tensor(
                    out=ABt[:, b5_bl + k * 128: b5_bl + k * 128 + 128],
                    in0=AFt[:, o5_x2 + k * 128: o5_x2 + k * 128 + 128],
                    in1=ABt[:, b5_bh + k * 128: b5_bh + k * 128 + 128], op=OP.subtract))
            p.wait("gpsimd", "DV")
            gdma(lambda h, of=of: h.dma_start(out=wip_hi[:, of * 2048:(of + 1) * 2048],
                                              in_=ABt[:, b5_bh:b5_bh + 2048]))
            gdma(lambda h, of=of: h.dma_start(out=wip_lo[:, of * 2048:(of + 1) * 2048],
                                              in_=ABt[:, b5_bl:b5_bl + 2048]))
        allbar()
        for tt in range(4):
            allbar()
            for k in range(KD):
                sy(lambda h, tt=tt, k=k: h.dma_start(
                    out=AFt[:, o5_x2 + k * 512: o5_x2 + k * 512 + 512],
                    in_=x2st[k * 128:(k + 1) * 128, tt * 512:(tt + 1) * 512]))
            p.wait("vector", "LD")
            for k in range(KD):
                dv(lambda h, k=k: h.tensor_copy(out=X2H[:, k * 512:(k + 1) * 512],
                                                in_=AFt[:, o5_x2 + k * 512: o5_x2 + k * 512 + 512]))
                dv(lambda h, k=k: h.tensor_tensor(out=X2L[:, k * 512:(k + 1) * 512],
                                                  in0=AFt[:, o5_x2 + k * 512: o5_x2 + k * 512 + 512],
                                                  in1=X2H[:, k * 512:(k + 1) * 512], op=OP.subtract))
            allbar()
            for of in range(8):
                sy(lambda h, of=of: h.dma_start(out=ABt[:, b5_bh:b5_bh + 2048],
                                                in_=wip_hi[:, of * 2048:(of + 1) * 2048]))
                sy(lambda h, of=of: h.dma_start(out=ABt[:, b5_bl:b5_bl + 2048],
                                                in_=wip_lo[:, of * 2048:(of + 1) * 2048]))
                allbar()
                for pi, (ws_, xs_) in enumerate((("h", "h"), ("h", "l"), ("l", "h"))):
                    for k in range(KD):
                        wof = (b5_bh if ws_ == "h" else b5_bl) + k * 128
                        xv = X2H if xs_ == "h" else X2L
                        te(lambda h, k=k, wof=wof, xv=xv,
                           first=(pi == 0 and k == 0), last=(pi == 2 and k == KD - 1):
                           h.matmul(out=PS[0][:, 0:512], lhsT=ABt[:, wof:wof + 128],
                                    rhs=xv[:, k * 512:(k + 1) * 512], start=first, stop=last))
                p.wait("vector", "MM")
                dv(lambda h, of=of: h.tensor_scalar(out=AFt[:, o5_hf:o5_hf + 512],
                                                    in0=PS[0][:, 0:512],
                                                    scalar1=BAt[:, of:of + 1],
                                                    scalar2=None, op0=OP.add))
                dv(lambda h: h.tensor_copy(out=ABt[:, b5_ah:b5_ah + 512],
                                           in_=AFt[:, o5_hf:o5_hf + 512]))
                dv(lambda h: h.tensor_tensor(out=ABt[:, b5_al:b5_al + 512],
                                             in0=AFt[:, o5_hf:o5_hf + 512],
                                             in1=ABt[:, b5_ah:b5_ah + 512], op=OP.subtract))
                p.wait("sync", "DV")
                syw(lambda h, tt=tt, of=of: h.dma_start(
                    out=hg_in[of * 128:(of + 1) * 128, tt * 512:(tt + 1) * 512],
                    in_=ABt[:, b5_ah:b5_ah + 512]))
                syw(lambda h, tt=tt, of=of: h.dma_start(
                    out=hg_in[1024 + of * 128:1024 + (of + 1) * 128, tt * 512:(tt + 1) * 512],
                    in_=ABt[:, b5_al:b5_al + 512]))
                allbar()
        allbar()
        cc_gather(hg_in, hg_out, PAIRS)
        allbar()
        hgv = [hg_out[s:s + 1].rearrange("o p n -> (o p) n") for s in range(2)]

        def load_h_tiles(tt8):
            s = tt8 // 4
            lc = (tt8 % 4) * 512
            for k in range(8):
                sy(lambda h, s=s, lc=lc, k=k: h.dma_start(
                    out=ABt[:, b5_bh + k * 512: b5_bh + k * 512 + 512],
                    in_=hgv[s][k * 128:(k + 1) * 128, lc:lc + 512]))
                sy(lambda h, s=s, lc=lc, k=k: h.dma_start(
                    out=ABt[:, b5_bl + k * 512: b5_bl + k * 512 + 512],
                    in_=hgv[s][1024 + k * 128:1024 + (k + 1) * 128, lc:lc + 512]))

        allbar()
        for k in range(8):
            sy(lambda h, k=k: h.dma_start(out=AFt[:, o5_kp + k * 64: o5_kp + k * 64 + 64],
                                          in_=w_b[k * 128:(k + 1) * 128, :]))
        p.wait("vector", "LD")
        for k in range(8):
            dv(lambda h, k=k: h.tensor_copy(out=ABt[:, b5_wh + k * 64: b5_wh + k * 64 + 64],
                                            in_=AFt[:, o5_kp + k * 64: o5_kp + k * 64 + 64]))
            dv(lambda h, k=k: h.tensor_tensor(out=ABt[:, b5_wl + k * 64: b5_wl + k * 64 + 64],
                                              in0=AFt[:, o5_kp + k * 64: o5_kp + k * 64 + 64],
                                              in1=ABt[:, b5_wh + k * 64: b5_wh + k * 64 + 64],
                                              op=OP.subtract))
        dv(lambda h: h.memset(ONESB, 1.0))
        allbar()
        for tt8 in range(8):
            load_h_tiles(tt8)
            allbar()
            for pi, (ws_, xs_) in enumerate((("h", "h"), ("h", "l"), ("l", "h"))):
                for k in range(8):
                    wof = (b5_wh if ws_ == "h" else b5_wl) + k * 64
                    xof = (b5_bh if xs_ == "h" else b5_bl) + k * 512
                    te(lambda h, wof=wof, xof=xof, first=(pi == 0 and k == 0),
                       last=(pi == 2 and k == 7): h.matmul(
                        out=PS[0][:64, 0:512], lhsT=ABt[:, wof:wof + 64],
                        rhs=ABt[:, xof:xof + 512], start=first, stop=last))
            p.wait("vector", "MM")
            dv(lambda h: h.tensor_scalar(out=AFt[:64, o5_hf:o5_hf + 512],
                                         in0=PS[0][:64, 0:512], scalar1=BBt[:, 0:1],
                                         scalar2=None, op0=OP.add))
            allbar()
            te(lambda h: h.matmul(out=PS[1][:64, 0:512], lhsT=PSIGN[:64, :64],
                                  rhs=AFt[:64, o5_hf:o5_hf + 512], start=True, stop=True))
            sy(lambda h, tt8=tt8: h.dma_start(out=AFt[:64, o5_cb:o5_cb + 512],
                                              in_=cos_k[:, tt8 * 512:(tt8 + 1) * 512]))
            sy(lambda h, tt8=tt8: h.dma_start(out=AFt[:64, o5_cb + 512:o5_cb + 1024],
                                              in_=sin_k[:, tt8 * 512:(tt8 + 1) * 512]))
            allbar()
            dv(lambda h: h.tensor_tensor(out=AFt[:64, o5_t1:o5_t1 + 512],
                                         in0=AFt[:64, o5_hf:o5_hf + 512],
                                         in1=AFt[:64, o5_cb:o5_cb + 512], op=OP.mult))
            dv(lambda h: h.tensor_tensor(out=AFt[:64, o5_rr:o5_rr + 512],
                                         in0=PS[1][:64, 0:512],
                                         in1=AFt[:64, o5_cb + 512:o5_cb + 1024], op=OP.mult))
            dv(lambda h, tt8=tt8: h.tensor_tensor(
                out=AFt[:64, o5_krtf + tt8 * 512: o5_krtf + tt8 * 512 + 512],
                in0=AFt[:64, o5_t1:o5_t1 + 512],
                in1=AFt[:64, o5_rr:o5_rr + 512], op=OP.add))
            allbar()

        for ih in range(NHC):
            allbar()
            for idx, wsrc in enumerate((w_uk, w_uv, w_uq)):
                for k in range(4):
                    sy(lambda h, wsrc=wsrc, k=k, ih=ih, idx=idx: h.dma_start(
                        out=AFt[:, o5_kp + idx * 512 + k * 128: o5_kp + idx * 512 + k * 128 + 128],
                        in_=wsrc[k * 128:(k + 1) * 128, ih * 128:(ih + 1) * 128]))
            p.wait("vector", "LD")
            for j in range(12):
                off = o5_kp + j * 128
                dv(lambda h, off=off, j=j: h.tensor_copy(
                    out=ABt[:, b5_wh + j * 128: b5_wh + j * 128 + 128],
                    in_=AFt[:, off:off + 128]))
                dv(lambda h, off=off, j=j: h.tensor_tensor(
                    out=ABt[:, b5_wl + j * 128: b5_wl + j * 128 + 128],
                    in0=AFt[:, off:off + 128],
                    in1=ABt[:, b5_wh + j * 128: b5_wh + j * 128 + 128], op=OP.subtract))
            allbar()
            for k in range(8):
                sy(lambda h, k=k, ih=ih: h.dma_start(
                    out=AFt[:, o5_kp + k * 64: o5_kp + k * 64 + 64],
                    in_=w_c[k * 128:(k + 1) * 128, ih * 64:(ih + 1) * 64]))
            p.wait("vector", "LD")
            for k in range(8):
                dv(lambda h, k=k: h.tensor_copy(out=ABt[:, b5_ch + k * 64: b5_ch + k * 64 + 64],
                                                in_=AFt[:, o5_kp + k * 64: o5_kp + k * 64 + 64]))
                dv(lambda h, k=k: h.tensor_tensor(out=ABt[:, b5_cl + k * 64: b5_cl + k * 64 + 64],
                                                  in0=AFt[:, o5_kp + k * 64: o5_kp + k * 64 + 64],
                                                  in1=ABt[:, b5_ch + k * 64: b5_ch + k * 64 + 64],
                                                  op=OP.subtract))
            allbar()
            for tt8 in range(8):
                load_h_tiles(tt8)
                allbar()
                for nm, wbase, klo, pb in (("k", 0, 0, 0), ("v", 512, 0, 1), ("q", 1024, 4, 2)):
                    for pi, (ws_, xs_) in enumerate((("h", "h"), ("h", "l"), ("l", "h"))):
                        for kk in range(4):
                            k = klo + kk
                            wof = (b5_wh if ws_ == "h" else b5_wl) + wbase + kk * 128
                            xof = (b5_bh if xs_ == "h" else b5_bl) + k * 512
                            te(lambda h, wof=wof, xof=xof, pb=pb,
                               first=(pi == 0 and kk == 0), last=(pi == 2 and kk == 3):
                               h.matmul(out=PS[pb][:, 0:512], lhsT=ABt[:, wof:wof + 128],
                                        rhs=ABt[:, xof:xof + 512], start=first, stop=last))
                for pi, (ws_, xs_) in enumerate((("h", "h"), ("h", "l"), ("l", "h"))):
                    for k in range(8):
                        wof = (b5_ch if ws_ == "h" else b5_cl) + k * 64
                        xof = (b5_bh if xs_ == "h" else b5_bl) + k * 512
                        te(lambda h, wof=wof, xof=xof, first=(pi == 0 and k == 0),
                           last=(pi == 2 and k == 7): h.matmul(
                            out=PS[3][:64, 0:512], lhsT=ABt[:, wof:wof + 64],
                            rhs=ABt[:, xof:xof + 512], start=first, stop=last))
                p.wait("vector", "MM")
                dv(lambda h, tt8=tt8: h.tensor_copy(out=QKV[:, KH + tt8 * 512: KH + tt8 * 512 + 512],
                                                    in_=PS[0][:, 0:512]))
                dv(lambda h, tt8=tt8: h.tensor_tensor(out=QKV[:, KL + tt8 * 512: KL + tt8 * 512 + 512],
                                                      in0=PS[0][:, 0:512],
                                                      in1=QKV[:, KH + tt8 * 512: KH + tt8 * 512 + 512],
                                                      op=OP.subtract))
                dv(lambda h, tt8=tt8: h.tensor_copy(out=QKV[:, QH + tt8 * 512: QH + tt8 * 512 + 512],
                                                    in_=PS[2][:, 0:512]))
                dv(lambda h, tt8=tt8: h.tensor_tensor(out=QKV[:, QL + tt8 * 512: QL + tt8 * 512 + 512],
                                                      in0=PS[2][:, 0:512],
                                                      in1=QKV[:, QH + tt8 * 512: QH + tt8 * 512 + 512],
                                                      op=OP.subtract))
                dv(lambda h: h.tensor_copy(out=AFt[:, o5_hf:o5_hf + 512], in_=PS[1][:, 0:512]))
                dv(lambda h, ih=ih: h.tensor_scalar(
                    out=AFt[:64, o5_t1:o5_t1 + 512], in0=PS[3][:64, 0:512],
                    scalar1=BCt[(ih % 2) * 64:(ih % 2) * 64 + 64, ih // 2:ih // 2 + 1],
                    scalar2=None, op0=OP.add))
                allbar()
                for tb in range(4):
                    te(lambda h, tb=tb: h.transpose(
                        out=PS[4][:, tb * 128:tb * 128 + 128],
                        in_=AFt[:, o5_hf + tb * 128: o5_hf + tb * 128 + 128],
                        identity=IDN[:, :]))
                te(lambda h: h.matmul(out=PS[5][:64, 0:512], lhsT=PSIGN[:64, :64],
                                      rhs=AFt[:64, o5_t1:o5_t1 + 512], start=True, stop=True))
                sy(lambda h, ih=ih, tt8=tt8: h.dma_start(
                    out=AFt[:64, o5_cb:o5_cb + 512],
                    in_=cos_q[ih * 64:(ih + 1) * 64, tt8 * 512:(tt8 + 1) * 512]))
                sy(lambda h, ih=ih, tt8=tt8: h.dma_start(
                    out=AFt[:64, o5_cb + 512:o5_cb + 1024],
                    in_=sin_q[ih * 64:(ih + 1) * 64, tt8 * 512:(tt8 + 1) * 512]))
                allbar()
                for tb in range(4):
                    kb = tt8 * 4 + tb
                    dv(lambda h, kb=kb, tb=tb: h.tensor_copy(
                        out=QKV[:, VTH + kb * 128: VTH + kb * 128 + 128],
                        in_=PS[4][:, tb * 128:tb * 128 + 128]))
                    dv(lambda h, kb=kb, tb=tb: h.tensor_tensor(
                        out=QKV[:, VTL + kb * 128: VTL + kb * 128 + 128],
                        in0=PS[4][:, tb * 128:tb * 128 + 128],
                        in1=QKV[:, VTH + kb * 128: VTH + kb * 128 + 128],
                        op=OP.subtract))
                dv(lambda h: h.tensor_tensor(out=AFt[:64, o5_rr:o5_rr + 512],
                                             in0=AFt[:64, o5_t1:o5_t1 + 512],
                                             in1=AFt[:64, o5_cb:o5_cb + 512], op=OP.mult))
                dv(lambda h, tt8=tt8: h.tensor_tensor(
                    out=AFt[:64, o5_qrtf + tt8 * 512: o5_qrtf + tt8 * 512 + 512],
                    in0=PS[5][:64, 0:512],
                    in1=AFt[:64, o5_cb + 512:o5_cb + 1024], op=OP.mult))
                dv(lambda h, tt8=tt8: h.tensor_tensor(
                    out=AFt[:64, o5_qrtf + tt8 * 512: o5_qrtf + tt8 * 512 + 512],
                    in0=AFt[:64, o5_qrtf + tt8 * 512: o5_qrtf + tt8 * 512 + 512],
                    in1=AFt[:64, o5_rr:o5_rr + 512], op=OP.add))
                allbar()
            for side in range(2):
                main = QH if side == 0 else KH
                for tt8 in range(8):
                    allbar()
                    ac(lambda h, main=main, tt8=tt8: h.activation(
                        out=AFt[:, o5_hf:o5_hf + 512],
                        in_=QKV[:, main + tt8 * 512: main + tt8 * 512 + 512], func=AF.Square))
                    if side == 0:
                        ac(lambda h, tt8=tt8: h.activation(
                            out=AFt[:64, o5_t1:o5_t1 + 512],
                            in_=AFt[:64, o5_qrtf + tt8 * 512: o5_qrtf + tt8 * 512 + 512],
                            func=AF.Square))
                    else:
                        ac(lambda h, tt8=tt8: h.activation(
                            out=AFt[:64, o5_t1:o5_t1 + 512],
                            in_=AFt[:64, o5_krtf + tt8 * 512: o5_krtf + tt8 * 512 + 512],
                            func=AF.Square))
                    p.wait("tensor", "AC")
                    te(lambda h: h.matmul(out=PS[6][0:1, 0:512], lhsT=ONESt[:, 0:1],
                                          rhs=AFt[:, o5_hf:o5_hf + 512],
                                          start=True, stop=False))
                    te(lambda h: h.matmul(out=PS[6][0:1, 0:512], lhsT=ONESt[:64, 0:1],
                                          rhs=AFt[:64, o5_t1:o5_t1 + 512],
                                          start=False, stop=True))
                    p.wait("vector", "MM")
                    dv(lambda h, side=side, tt8=tt8: h.tensor_reduce(
                        out=AFt[0:1, o5_row + side * 8 + tt8: o5_row + side * 8 + tt8 + 1],
                        in_=PS[6][0:1, 0:512], op=OP.max, axis=AX.X))
                allbar()
                dv(lambda h, side=side: h.tensor_reduce(
                    out=AFt[0:1, o5_row + 20 + side:o5_row + 21 + side],
                    in_=AFt[0:1, o5_row + side * 8:o5_row + side * 8 + 8],
                    op=OP.max, axis=AX.X))
            allbar()
            dv(lambda h: h.tensor_tensor(out=AFt[0:1, o5_row + 22:o5_row + 23],
                                         in0=AFt[0:1, o5_row + 20:o5_row + 21],
                                         in1=AFt[0:1, o5_row + 21:o5_row + 22], op=OP.mult))
            p.wait("scalar", "DV")
            ac(lambda h: h.activation(out=AFt[0:1, o5_row + 23:o5_row + 24],
                                      in_=AFt[0:1, o5_row + 22:o5_row + 23], func=AF.Sqrt))
            p.wait("vector", "AC")
            dv(lambda h: h.tensor_scalar(out=AFt[0:1, o5_row + 24:o5_row + 25],
                                         in0=AFt[0:1, o5_row + 23:o5_row + 24],
                                         scalar1=-SCALE_DK, scalar2=None, op0=OP.mult))
            allbar()
            k1_bcast(AFt[0:1, o5_row + 24:o5_row + 25], PS[5], 1)
            dv(lambda h: h.tensor_copy(out=AFt[:, o5_row + 16:o5_row + 17],
                                       in_=PS[5][:, 0:1]))
            allbar()
            NEGB = AFt[:, o5_row + 16:o5_row + 17]
            for qt in range(8):
                nkb = 4 * qt + 4
                for kb in range(nkb):
                    p.wait("tensor", "AC")
                    for pi, (ks_, qs_) in enumerate((("h", "h"), ("h", "l"), ("l", "h"))):
                        ko = KH if ks_ == "h" else KL
                        qo = QH if qs_ == "h" else QL
                        te(lambda h, kb=kb, qt=qt, ko=ko, qo=qo, first=(pi == 0):
                           h.matmul(out=PS[0][:, 0:512],
                                    lhsT=QKV[:, ko + kb * 128: ko + kb * 128 + 128],
                                    rhs=QKV[:, qo + qt * 512: qo + qt * 512 + 512],
                                    start=first, stop=False))
                    te(lambda h, kb=kb, qt=qt: h.matmul(
                        out=PS[0][:, 0:512],
                        lhsT=AFt[:64, o5_krtf + kb * 128: o5_krtf + kb * 128 + 128],
                        rhs=AFt[:64, o5_qrtf + qt * 512: o5_qrtf + qt * 512 + 512],
                        start=False, stop=True))
                    p.wait("scalar", "MM")
                    p.wait("scalar", "DV")
                    ac(lambda h: h.activation(out=AFt[:, o5_ao:o5_ao + 512],
                                              in_=PS[0][:, 0:512],
                                              func=AF.Exp, scale=SCALE_DK, bias=NEGB))
                    p.wait("vector", "AC")
                    if kb >= 4 * qt:
                        dv(lambda h, kb=kb, qt=qt: h.tensor_tensor(
                            out=AFt[:, o5_ao:o5_ao + 512], in0=AFt[:, o5_ao:o5_ao + 512],
                            in1=AMASK[:, (kb - 4 * qt) * 512:(kb - 4 * qt + 1) * 512],
                            op=OP.mult))
                    p.wait("vector", "MM")
                    dv(lambda h: h.tensor_copy(out=ABt[:, b5_ah:b5_ah + 512],
                                               in_=AFt[:, o5_ao:o5_ao + 512]))
                    dv(lambda h: h.tensor_tensor(out=ABt[:, b5_al:b5_al + 512],
                                                 in0=AFt[:, o5_ao:o5_ao + 512],
                                                 in1=ABt[:, b5_ah:b5_ah + 512], op=OP.subtract))
                    p.wait("tensor", "DV")
                    te(lambda h, kb=kb: h.matmul(out=PS[2][0:1, 0:512],
                                                 lhsT=ONESB[:, 0:1],
                                                 rhs=ABt[:, b5_ah:b5_ah + 512],
                                                 start=(kb == 0), stop=False))
                    te(lambda h, kb=kb, nkb=nkb: h.matmul(out=PS[2][0:1, 0:512],
                                                          lhsT=ONESB[:, 0:1],
                                                          rhs=ABt[:, b5_al:b5_al + 512],
                                                          start=False, stop=(kb == nkb - 1)))
                    for pi, (vs_, as_) in enumerate((("h", "h"), ("h", "l"), ("l", "h"))):
                        vo = VTH if vs_ == "h" else VTL
                        ao_ = b5_ah if as_ == "h" else b5_al
                        te(lambda h, kb=kb, vo=vo, ao_=ao_, first=(pi == 0 and kb == 0),
                           last=(pi == 2 and kb == nkb - 1): h.matmul(
                            out=PS[3][:, 0:512],
                            lhsT=QKV[:, vo + kb * 128: vo + kb * 128 + 128],
                            rhs=ABt[:, ao_:ao_ + 512], start=first, stop=last))
                allbar()
                dv(lambda h: h.tensor_copy(out=AFt[0:1, o5_row + 32:o5_row + 544],
                                           in_=PS[2][0:1, 0:512]))
                dv(lambda h: h.reciprocal(out=AFt[0:1, o5_row + 576:o5_row + 1088],
                                          in_=AFt[0:1, o5_row + 32:o5_row + 544]))
                dv(lambda h: h.tensor_tensor(out=AFt[0:1, o5_row + 1088:o5_row + 1600],
                                             in0=AFt[0:1, o5_row + 32:o5_row + 544],
                                             in1=AFt[0:1, o5_row + 576:o5_row + 1088],
                                             op=OP.mult))
                dv(lambda h: h.tensor_scalar(out=AFt[0:1, o5_row + 1088:o5_row + 1600],
                                             in0=AFt[0:1, o5_row + 1088:o5_row + 1600],
                                             scalar1=-1.0, scalar2=2.0,
                                             op0=OP.mult, op1=OP.add))
                dv(lambda h: h.tensor_tensor(out=AFt[0:1, o5_row + 576:o5_row + 1088],
                                             in0=AFt[0:1, o5_row + 576:o5_row + 1088],
                                             in1=AFt[0:1, o5_row + 1088:o5_row + 1600],
                                             op=OP.mult))
                allbar()
                k1_bcast(AFt[0:1, o5_row + 576:o5_row + 1088], PS[4], 512)
                dv(lambda h: h.tensor_copy(out=AFt[:, o5_rb:o5_rb + 512], in_=PS[4][:, 0:512]))
                allbar()
                dv(lambda h: h.tensor_tensor(out=AFt[:, o5_ao:o5_ao + 512],
                                             in0=PS[3][:, 0:512],
                                             in1=AFt[:, o5_rb:o5_rb + 512], op=OP.mult))
                dv(lambda h: h.tensor_copy(out=ABt[:, b5_ah:b5_ah + 512],
                                           in_=AFt[:, o5_ao:o5_ao + 512]))
                dv(lambda h: h.tensor_tensor(out=ABt[:, b5_al:b5_al + 512],
                                             in0=AFt[:, o5_ao:o5_ao + 512],
                                             in1=ABt[:, b5_ah:b5_ah + 512], op=OP.subtract))
                p.wait("sync", "DV")
                syw(lambda h, ih=ih, qt=qt: h.dma_start(
                    out=aost_hi[ih * 128:(ih + 1) * 128, qt * 512:(qt + 1) * 512],
                    in_=ABt[:, b5_ah:b5_ah + 512]))
                syw(lambda h, ih=ih, qt=qt: h.dma_start(
                    out=aost_lo[ih * 128:(ih + 1) * 128, qt * 512:(qt + 1) * 512],
                    in_=ABt[:, b5_al:b5_al + 512]))
                allbar()
        allbar()
        ao3_hi = ao_hi.rearrange("(g p) n -> g p n", p=128)
        ao3_lo = ao_lo.rearrange("(g p) n -> g p n", p=128)

        def e_own(h, src_t=None, dst3=None):
            tb = p.val("tokbase", "sync")
            rbk = p.val("rowblk", "sync")
            return h.dma_start(
                out=dst3[bass.ds(rbk, 8)].rearrange("g p n -> (g p) n"),
                in_=src_t[0:NHC * DH, bass.ds(tb, NT)])
        syw(lambda h: e_own(h, src_t=aost_hi, dst3=ao3_hi))
        syw(lambda h: e_own(h, src_t=aost_lo, dst3=ao3_lo))

        def e_part(h, src_t=None, hil=0):
            ptb = p.val("ptokbase", "sync")
            return h.dma_start(
                out=ag_in[hil * 1024:(hil + 1) * 1024, :],
                in_=src_t[0:NHC * DH, bass.ds(ptb, NT)])
        syw(lambda h: e_part(h, src_t=aost_hi, hil=0))
        syw(lambda h: e_part(h, src_t=aost_lo, hil=1))
        p.wait("sync", "ST")
        allbar()
        cc_gather(ag_in, ag_out, PAIRS)
        allbar()

        def e_recv(h):
            ps_ = p.val("pslot", "sync")
            return h.dma_start(
                out=agst[:, :],
                in_=ag_out[bass.ds(ps_, 1)].rearrange("o p x -> (o p) x"))
        syw(e_recv)
        p.wait("sync", "ST")

        def e_place(h, hil=0, dst3=None):
            prbk = p.val("prowblk", "sync")
            return h.dma_start(
                out=dst3[bass.ds(prbk, 8)].rearrange("g p n -> (g p) n"),
                in_=agst[hil * 1024:(hil + 1) * 1024, :])
        syw(lambda h: e_place(h, hil=0, dst3=ao3_hi))
        syw(lambda h: e_place(h, hil=1, dst3=ao3_lo))
        p.wait("sync", "ST")
        allbar()
        for kk in range(16):
            allbar()
            sy(lambda h, kk=kk: h.dma_start(out=AFt[:, oU:oU + 2048],
                                            in_=w_z[kk * 128:(kk + 1) * 128, :]))
            p.wait("vector", "LD")
            dv(lambda h: h.tensor_copy(out=ABt[:, b5_bh:b5_bh + 2048],
                                       in_=AFt[:, oU:oU + 2048]))
            dv(lambda h: h.tensor_tensor(out=ABt[:, b5_bl:b5_bl + 2048],
                                         in0=AFt[:, oU:oU + 2048],
                                         in1=ABt[:, b5_bh:b5_bh + 2048], op=OP.subtract))
            p.wait("gpsimd", "DV")
            for od in range(16):
                gdma(lambda h, kk=kk, od=od: h.dma_start(
                    out=wip_hi[:, od * 2048 + kk * 128: od * 2048 + kk * 128 + 128],
                    in_=ABt[:, b5_bh + od * 128: b5_bh + od * 128 + 128]))
                gdma(lambda h, kk=kk, od=od: h.dma_start(
                    out=wip_lo[:, od * 2048 + kk * 128: od * 2048 + kk * 128 + 128],
                    in_=ABt[:, b5_bl + od * 128: b5_bl + od * 128 + 128]))
        allbar()
        fa5f = Ar(27300)
        f_ao = fa5f.a(8192)
        f_x3 = fa5f.a(8192)
        f_xi = fa5f.a(512)
        f_sq = fa5f.a(512)
        f_row = fa5f.a(3072)
        f_ab = fa5f.a(1024)
        assert fa5f.off <= 27300
        AOH = AFt[:, f_ao:f_ao + 4096].bitcast(BF16)
        AOL = AFt[:, f_ao + 4096:f_ao + 8192].bitcast(BF16)
        for tt in range(4):
            allbar()
            for k in range(16):
                sy(lambda h, tt=tt, k=k: h.dma_start(
                    out=AOH[:, k * 512:(k + 1) * 512],
                    in_=ao_hi[k * 128:(k + 1) * 128, tt * 512:(tt + 1) * 512]))
                sy(lambda h, tt=tt, k=k: h.dma_start(
                    out=AOL[:, k * 512:(k + 1) * 512],
                    in_=ao_lo[k * 128:(k + 1) * 128, tt * 512:(tt + 1) * 512]))
            allbar()
            for od in range(16):
                sy(lambda h, od=od: h.dma_start(
                    out=ABt[:, b5_bh:b5_bh + 2048],
                    in_=wip_hi[:, od * 2048:(od + 1) * 2048]))
                sy(lambda h, od=od: h.dma_start(
                    out=ABt[:, b5_bl:b5_bl + 2048],
                    in_=wip_lo[:, od * 2048:(od + 1) * 2048]))
                sy(lambda h, tt=tt, od=od: h.dma_start(
                    out=AFt[:, f_xi:f_xi + 512],
                    in_=x2st[od * 128:(od + 1) * 128, tt * 512:(tt + 1) * 512]))
                allbar()
                for pi, (ws_, xs_) in enumerate((("h", "h"), ("h", "l"), ("l", "h"))):
                    for k in range(16):
                        wof = (b5_bh if ws_ == "h" else b5_bl) + k * 128
                        xv = AOH if xs_ == "h" else AOL
                        te(lambda h, k=k, wof=wof, xv=xv,
                           first=(pi == 0 and k == 0), last=(pi == 2 and k == 15):
                           h.matmul(out=PS[0][:, 0:512], lhsT=ABt[:, wof:wof + 128],
                                    rhs=xv[:, k * 512:(k + 1) * 512], start=first, stop=last))
                p.wait("vector", "MM")
                dv(lambda h, od=od: h.scalar_tensor_tensor(
                    out=AFt[:, f_x3 + od * 512: f_x3 + od * 512 + 512],
                    in0=PS[0][:, 0:512], scalar=BZt[:, od:od + 1],
                    in1=AFt[:, f_xi:f_xi + 512], op0=OP.add, op1=OP.add))
                p.wait("scalar", "DV")
                ac(lambda h, od=od: h.activation(out=AFt[:, f_sq:f_sq + 512],
                                                 in_=AFt[:, f_x3 + od * 512: f_x3 + od * 512 + 512],
                                                 func=AF.Square))
                p.wait("tensor", "AC")
                p.wait("tensor", "DV")
                te(lambda h, od=od: h.matmul(out=PS[6][0:1, 0:512], lhsT=ONESt[:, 0:1],
                                             rhs=AFt[:, f_x3 + od * 512: f_x3 + od * 512 + 512],
                                             start=(od == 0), stop=(od == 15)))
                te(lambda h, od=od: h.matmul(out=PS[7][0:1, 0:512], lhsT=ONESt[:, 0:1],
                                             rhs=AFt[:, f_sq:f_sq + 512],
                                             start=(od == 0), stop=(od == 15)))
                allbar()
            ln_finalize(PS[6], PS[7], DIM, 512, f_row, f_ab, PS[5])
            allbar()
            for od in range(16):
                dv(lambda h, od=od: h.tensor_tensor(
                    out=AFt[:, f_x3 + od * 512: f_x3 + od * 512 + 512],
                    in0=AFt[:, f_x3 + od * 512: f_x3 + od * 512 + 512],
                    in1=AFt[:, f_ab:f_ab + 512], op=OP.mult))
                dv(lambda h, od=od: h.tensor_tensor(
                    out=AFt[:, f_x3 + od * 512: f_x3 + od * 512 + 512],
                    in0=AFt[:, f_x3 + od * 512: f_x3 + od * 512 + 512],
                    in1=AFt[:, f_ab + 512:f_ab + 1024], op=OP.add))
                p.wait("sync", "DV")
                syw(lambda h, tt=tt, od=od: h.dma_start(
                    out=x3st[od * 128:(od + 1) * 128, tt * 512:(tt + 1) * 512],
                    in_=AFt[:, f_x3 + od * 512: f_x3 + od * 512 + 512]))
            allbar()

        fa6 = Ar(27300)
        o6_x3 = fa6.a(8192)
        o6_x4 = fa6.a(8192)
        o6_sq = fa6.a(512)
        o6_row = fa6.a(3200)
        o6_ab = fa6.a(1024)
        o6_mx = fa6.a(16)
        o6_y = fa6.a(512)
        o6_scl = fa6.a(24)
        assert fa6.off <= 27300
        b6_w = 0
        b6_y = 4096
        W8 = ABt[:, 2048:3072].bitcast(FP8)

        allbar()
        dv(lambda h: h.memset(AFt[:, o6_mx:o6_mx + 1], -1e30))
        for k in range(16):
            sy(lambda h, k=k: h.dma_start(out=ABt[:, b6_w:b6_w + 2048],
                                          in_=w_l1f[k * 128:(k + 1) * 128, :]))
            allbar()
            dv(lambda h: h.tensor_reduce(out=AFt[:, o6_mx + 2:o6_mx + 3],
                                         in_=ABt[:, b6_w:b6_w + 2048], op=OP.max, axis=AX.X))
            dv(lambda h: h.tensor_reduce(out=AFt[:, o6_mx + 3:o6_mx + 4],
                                         in_=ABt[:, b6_w:b6_w + 2048], op=OP.min, axis=AX.X))
            p.wait("vector", "DV")
            dv(lambda h: h.tensor_scalar(out=AFt[:, o6_mx + 3:o6_mx + 4],
                                         in0=AFt[:, o6_mx + 3:o6_mx + 4],
                                         scalar1=-1.0, scalar2=None, op0=OP.mult))
            p.wait("vector", "DV")
            dv(lambda h: h.tensor_tensor(out=AFt[:, o6_mx + 2:o6_mx + 3],
                                         in0=AFt[:, o6_mx + 2:o6_mx + 3],
                                         in1=AFt[:, o6_mx + 3:o6_mx + 4], op=OP.max))
            p.wait("vector", "DV")
            dv(lambda h: h.tensor_tensor(out=AFt[:, o6_mx:o6_mx + 1],
                                         in0=AFt[:, o6_mx:o6_mx + 1],
                                         in1=AFt[:, o6_mx + 2:o6_mx + 3], op=OP.max))
            allbar()
        te(lambda h: h.transpose(out=PS[5][0:1, 0:128], in_=AFt[:, o6_mx:o6_mx + 1],
                                 identity=IDN[:, :]))
        p.wait("vector", "MM")
        dv(lambda h: h.tensor_reduce(out=AFt[0:1, o6_mx + 4:o6_mx + 5], in_=PS[5][0:1, 0:128],
                                     op=OP.max, axis=AX.X))
        allbar()
        dv(lambda h: h.memset(AFt[:, o6_mx + 8:o6_mx + 9], -1e30))
        for tt in range(4):
            allbar()
            for k in range(16):
                sy(lambda h, tt=tt, k=k: h.dma_start(
                    out=AFt[:, o6_x3 + k * 512: o6_x3 + k * 512 + 512],
                    in_=x3st[k * 128:(k + 1) * 128, tt * 512:(tt + 1) * 512]))
            allbar()
            for k in range(16):
                dv(lambda h, k=k: h.tensor_reduce(out=AFt[:, o6_mx + 9:o6_mx + 10],
                                                  in_=AFt[:, o6_x3 + k * 512: o6_x3 + k * 512 + 512],
                                                  op=OP.max, axis=AX.X))
                dv(lambda h, k=k: h.tensor_reduce(out=AFt[:, o6_mx + 10:o6_mx + 11],
                                                  in_=AFt[:, o6_x3 + k * 512: o6_x3 + k * 512 + 512],
                                                  op=OP.min, axis=AX.X))
                p.wait("vector", "DV")
                dv(lambda h: h.tensor_scalar(out=AFt[:, o6_mx + 10:o6_mx + 11],
                                             in0=AFt[:, o6_mx + 10:o6_mx + 11],
                                             scalar1=-1.0, scalar2=None, op0=OP.mult))
                p.wait("vector", "DV")
                dv(lambda h: h.tensor_tensor(out=AFt[:, o6_mx + 9:o6_mx + 10],
                                             in0=AFt[:, o6_mx + 9:o6_mx + 10],
                                             in1=AFt[:, o6_mx + 10:o6_mx + 11], op=OP.max))
                p.wait("vector", "DV")
                dv(lambda h: h.tensor_tensor(out=AFt[:, o6_mx + 8:o6_mx + 9],
                                             in0=AFt[:, o6_mx + 8:o6_mx + 9],
                                             in1=AFt[:, o6_mx + 9:o6_mx + 10], op=OP.max))
        allbar()
        te(lambda h: h.transpose(out=PS[5][0:1, 0:128], in_=AFt[:, o6_mx + 8:o6_mx + 9],
                                 identity=IDN[:, :]))
        p.wait("vector", "MM")
        dv(lambda h: h.tensor_reduce(out=AFt[0:1, o6_mx + 11:o6_mx + 12], in_=PS[5][0:1, 0:128],
                                     op=OP.max, axis=AX.X))
        p.wait("vector", "DV")
        dv(lambda h: h.tensor_scalar(out=AFt[0:1, o6_mx + 11:o6_mx + 12],
                                     in0=AFt[0:1, o6_mx + 11:o6_mx + 12],
                                     scalar1=1e-6, scalar2=None, op0=OP.max))
        dv(lambda h: h.tensor_scalar(out=AFt[0:1, o6_mx + 4:o6_mx + 5],
                                     in0=AFt[0:1, o6_mx + 4:o6_mx + 5],
                                     scalar1=1e-6, scalar2=None, op0=OP.max))
        p.wait("vector", "DV")
        p.wait("gpsimd", "DV")
        gdma(lambda h: h.dma_start(out=mx_in, in_=AFt[0:1, o6_mx + 11:o6_mx + 12]))
        p.wait("gpsimd", "LG")
        p.emit("gpsimd", lambda h: h.collective_compute(
            "AllReduce", OP.max, replica_groups=[list(range(8))],
            ins=[mx_in], outs=[mx_out]), inc="CC")
        p.wait("gpsimd", "CC")
        p.bump("gpsimd", "GP", 1)
        p.wait("sync", "GP")
        sy(lambda h: h.dma_start(out=AFt[0:1, o6_mx + 12:o6_mx + 13], in_=mx_out))
        allbar()
        for src_o, dst_o in ((o6_mx + 12, o6_mx + 13), (o6_mx + 4, o6_mx + 5)):
            dv(lambda h, src_o=src_o, dst_o=dst_o: h.reciprocal(
                out=AFt[0:1, dst_o:dst_o + 1], in_=AFt[0:1, src_o:src_o + 1]))
            p.wait("vector", "DV")
            dv(lambda h, src_o=src_o, dst_o=dst_o: h.tensor_tensor(
                out=AFt[0:1, o6_mx + 6:o6_mx + 7], in0=AFt[0:1, src_o:src_o + 1],
                in1=AFt[0:1, dst_o:dst_o + 1], op=OP.mult))
            p.wait("vector", "DV")
            dv(lambda h: h.tensor_scalar(out=AFt[0:1, o6_mx + 6:o6_mx + 7],
                                         in0=AFt[0:1, o6_mx + 6:o6_mx + 7],
                                         scalar1=-1.0, scalar2=2.0, op0=OP.mult, op1=OP.add))
            p.wait("vector", "DV")
            dv(lambda h, dst_o=dst_o: h.tensor_tensor(
                out=AFt[0:1, dst_o:dst_o + 1], in0=AFt[0:1, dst_o:dst_o + 1],
                in1=AFt[0:1, o6_mx + 6:o6_mx + 7], op=OP.mult))
            p.wait("vector", "DV")
        dv(lambda h: h.tensor_tensor(out=AFt[0:1, o6_mx + 14:o6_mx + 15],
                                     in0=AFt[0:1, o6_mx + 12:o6_mx + 13],
                                     in1=AFt[0:1, o6_mx + 4:o6_mx + 5], op=OP.mult))
        allbar()
        for j, src_o in enumerate((o6_mx + 13, o6_mx + 5, o6_mx + 14)):
            k1_bcast(AFt[0:1, src_o:src_o + 1], PS[5], 1)
            dv(lambda h, j=j: h.tensor_copy(out=AFt[:, o6_scl + j * 8:o6_scl + j * 8 + 1],
                                            in_=PS[5][:, 0:1]))
            allbar()
        INVXS = AFt[:, o6_scl:o6_scl + 1]
        INVWS = AFt[:, o6_scl + 8:o6_scl + 9]
        XSWS = AFt[:, o6_scl + 16:o6_scl + 17]
        for k in range(16):
            sy(lambda h, k=k: h.dma_start(out=ABt[:, b6_w:b6_w + 2048],
                                          in_=w_l1f[k * 128:(k + 1) * 128, :]))
            allbar()
            dv(lambda h: h.tensor_scalar(out=AQt[:, 0:2048], in0=ABt[:, b6_w:b6_w + 2048],
                                         scalar1=INVWS, scalar2=None, op0=OP.mult))
            p.wait("sync", "DV")
            syw(lambda h, k=k: h.dma_start(out=wq8[k * 128:(k + 1) * 128, :],
                                           in_=AQt[:, 0:2048]))
            allbar()
        for tt in range(4):
            allbar()
            for k in range(16):
                sy(lambda h, tt=tt, k=k: h.dma_start(
                    out=AFt[:, o6_x3 + k * 512: o6_x3 + k * 512 + 512],
                    in_=x3st[k * 128:(k + 1) * 128, tt * 512:(tt + 1) * 512]))
            allbar()
            for k in range(16):
                dv(lambda h, k=k: h.tensor_scalar(
                    out=AQt[:, k * 512:(k + 1) * 512],
                    in0=AFt[:, o6_x3 + k * 512: o6_x3 + k * 512 + 512],
                    scalar1=INVXS, scalar2=None, op0=OP.mult))
            allbar()
            for od in range(16):
                for k in range(16):
                    sy(lambda h, od=od, k=k: h.dma_start(
                        out=W8[:, k * 128:(k + 1) * 128],
                        in_=wq8[k * 128:(k + 1) * 128, od * 128:(od + 1) * 128]))
                allbar()
                for k in range(16):
                    te(lambda h, k=k, first=(k == 0), last=(k == 15): h.matmul(
                        out=PS[0][:, 0:512], lhsT=W8[:, k * 128:(k + 1) * 128],
                        rhs=AQt[:, k * 512:(k + 1) * 512], start=first, stop=last))
                p.wait("vector", "MM")
                dv(lambda h: h.tensor_scalar(out=AFt[:, o6_y:o6_y + 512], in0=PS[0][:, 0:512],
                                             scalar1=XSWS, scalar2=None, op0=OP.mult))
                dv(lambda h: h.tensor_copy(out=ABt[:, b6_y:b6_y + 512],
                                           in_=AFt[:, o6_y:o6_y + 512]))
                dv(lambda h, od=od: h.tensor_scalar(out=ABt[:, b6_y + 512:b6_y + 1024],
                                                    in0=ABt[:, b6_y:b6_y + 512],
                                                    scalar1=BL1[:, od:od + 1],
                                                    scalar2=None, op0=OP.add))
                dv(lambda h, od=od: h.tensor_tensor(
                    out=AFt[:, o6_x4 + od * 512: o6_x4 + od * 512 + 512],
                    in0=AFt[:, o6_x3 + od * 512: o6_x3 + od * 512 + 512],
                    in1=ABt[:, b6_y + 512:b6_y + 1024], op=OP.add))
                p.wait("scalar", "DV")
                ac(lambda h, od=od: h.activation(out=AFt[:, o6_sq:o6_sq + 512],
                                                 in_=AFt[:, o6_x4 + od * 512: o6_x4 + od * 512 + 512],
                                                 func=AF.Square))
                p.wait("tensor", "AC")
                p.wait("tensor", "DV")
                te(lambda h, od=od: h.matmul(out=PS[6][0:1, 0:512], lhsT=ONESt[:, 0:1],
                                             rhs=AFt[:, o6_x4 + od * 512: o6_x4 + od * 512 + 512],
                                             start=(od == 0), stop=(od == 15)))
                te(lambda h, od=od: h.matmul(out=PS[7][0:1, 0:512], lhsT=ONESt[:, 0:1],
                                             rhs=AFt[:, o6_sq:o6_sq + 512],
                                             start=(od == 0), stop=(od == 15)))
                allbar()
            ln_finalize(PS[6], PS[7], DIM, 512, o6_row, o6_ab, PS[5])
            allbar()
            for od in range(16):
                dv(lambda h, od=od: h.tensor_tensor(
                    out=AFt[:, o6_x4 + od * 512: o6_x4 + od * 512 + 512],
                    in0=AFt[:, o6_x4 + od * 512: o6_x4 + od * 512 + 512],
                    in1=AFt[:, o6_ab:o6_ab + 512], op=OP.mult))
                dv(lambda h, od=od: h.tensor_tensor(
                    out=AFt[:, o6_x4 + od * 512: o6_x4 + od * 512 + 512],
                    in0=AFt[:, o6_x4 + od * 512: o6_x4 + od * 512 + 512],
                    in1=AFt[:, o6_ab + 512:o6_ab + 1024], op=OP.add))
                p.wait("sync", "DV")
                syw(lambda h, tt=tt, od=od: h.dma_start(
                    out=out[od * 128:(od + 1) * 128, tt * 512:(tt + 1) * 512],
                    in_=AFt[:, o6_x4 + od * 512: o6_x4 + od * 512 + 512]))
            allbar()

    if stop_after != "all":
        p.bar("sync")
        syw(lambda h: h.dma_start(out=out[0:128, 0:16], in_=ONESt[:, 0:16]))
    p.wait("sync", "ST")
    p.bar("sync")

    with nc.Block() as block:
        @block.sync
        def _(eng):
            for e, fn in p.items:
                if e == "sync":
                    fn(eng)

        @block.tensor
        def _(eng):
            for e, fn in p.items:
                if e == "tensor":
                    fn(eng)

        @block.vector
        def _(eng):
            for e, fn in p.items:
                if e == "vector":
                    fn(eng)

        @block.scalar
        def _(eng):
            for e, fn in p.items:
                if e == "scalar":
                    fn(eng)

        @block.gpsimd
        def _(eng):
            for e, fn in p.items:
                if e == "gpsimd":
                    fn(eng)

    return nc, st



def _prep_inputs(inputs):
    f32 = np.float32
    x = np.asarray(inputs["L0_attn_x"])
    ins = []

    def rope_tables(d):
        pos = np.repeat(np.arange(d // 2, dtype=np.float64), 2)
        theta = np.exp(-pos / d * np.log(10000.0)).astype(f32)
        freq = np.arange(1, T + 1, dtype=f32)
        ang = freq[:, None] * theta[None, :]
        return np.cos(ang).astype(f32), np.sin(ang).astype(f32)

    cosk, sink = rope_tables(DHR)
    cosq_full, sinq_full = rope_tables(NH * DHR)
    psign = np.zeros((128, 128), f32)
    for i in range(64):
        psign[2 * i + 1, 2 * i] = -1.0
        psign[2 * i, 2 * i + 1] = 1.0

    def fm(a):
        return np.ascontiguousarray(a, dtype=f32)

    w_inproj = fm(np.asarray(inputs["L0_attn_in_proj_w"]).T)
    conv_w = np.asarray(inputs["L0_attn_conv_w"], dtype=f32)
    conv_w_cm = fm(conv_w.reshape(FC, 128, CONV).transpose(1, 0, 2).reshape(128, FC * CONV))
    d_expand = fm(np.repeat(np.asarray(inputs["L0_attn_D"], dtype=f32), HEAD_DIM)
                  .reshape(32, 128).T)
    normw_cm = fm(np.asarray(inputs["L0_attn_norm_w"], dtype=f32).reshape(32, 128).T)
    w_outproj = fm(np.asarray(inputs["L0_attn_out_proj_w"]).T)
    def strip_major(w):
        a = np.ascontiguousarray(np.asarray(w))
        return np.ascontiguousarray(
            a.reshape(16, 128, 64, 128).transpose(1, 2, 0, 3).reshape(128, 131072))
    wg_cm = strip_major(inputs["L0_ffn_w_gate"])
    wu_cm = strip_major(inputs["L0_ffn_w_up"])
    sg = fm(np.asarray(inputs["L0_ffn_s_gate"], dtype=f32).reshape(64, 128).T)
    su = fm(np.asarray(inputs["L0_ffn_s_up"], dtype=f32).reshape(64, 128).T)
    sd = fm(np.asarray(inputs["L0_ffn_s_down"], dtype=f32).reshape(16, 128).T)
    w_a = fm(np.asarray(inputs["L1_attn_WA"]))
    b_a = fm(np.asarray(inputs["L1_attn_bA"], dtype=f32).reshape(8, 128).T)
    w_b = fm(np.asarray(inputs["L1_attn_WB"]))
    b_b = fm(np.asarray(inputs["L1_attn_bB"], dtype=f32).reshape(DHR, 1))
    w_z = fm(np.asarray(inputs["L1_attn_WZ"]))
    b_z = fm(np.asarray(inputs["L1_attn_bZ"], dtype=f32).reshape(16, 128).T)
    w_l1f = np.ascontiguousarray(np.asarray(inputs["L1_ffn_weight"]).T)
    b_l1f = np.ascontiguousarray(np.asarray(inputs["L1_ffn_bias"]).reshape(16, 128).T)
    dtb = fm(np.asarray(inputs["L0_attn_dt_bias"], dtype=f32).reshape(HEADS, 1))
    alog = fm(np.asarray(inputs["L0_attn_A_log"], dtype=f32).reshape(HEADS, 1))

    for core in range(8):
        b_, m_ = core // 2, core % 2
        t0 = NT * m_
        xs = x[b_, t0:t0 + NT, :]
        pad = np.zeros((3, DIM), f32) if m_ == 0 else np.asarray(x[b_, t0 - 3:t0, :], f32)
        xslab = fm(np.concatenate([pad, xs], axis=0).T)
        gh0 = m_ * NHC
        hc = slice(gh0 * DHR, (gh0 + NHC) * DHR)
        hq = slice(gh0 * DH, (gh0 + NHC) * DH)
        im = {
            "xin": xslab,
            "w_inproj": w_inproj,
            "conv_w": conv_w_cm,
            "dt_bias": dtb,
            "a_log": alog,
            "d_exp": d_expand,
            "norm_w": normw_cm,
            "w_outproj": w_outproj,
            "w_gate": wg_cm, "w_up": wu_cm,
            "w_down": np.ascontiguousarray(np.asarray(inputs["L0_ffn_w_down"])),
            "s_gate": sg, "s_up": su, "s_down": sd,
            "w_a": w_a, "b_a": b_a, "w_b": w_b, "b_b": b_b,
            "w_c": fm(np.asarray(inputs["L1_attn_WC"])[:, hc]),
            "b_c": fm(np.asarray(inputs["L1_attn_bC"], dtype=f32)[hc].reshape(4, 128).T),
            "w_uq": fm(np.asarray(inputs["L1_attn_WUQ"])[:, hq]),
            "w_uk": fm(np.asarray(inputs["L1_attn_WUK"])[:, hq]),
            "w_uv": fm(np.asarray(inputs["L1_attn_WUV"])[:, hq]),
            "w_z": w_z, "b_z": b_z,
            "w_l1f": w_l1f, "b_l1f": b_l1f,
            "cos_k": fm(cosk.T), "sin_k": fm(sink.T),
            "cos_q": fm(cosq_full.T[hc, :]), "sin_q": fm(sinq_full.T[hc, :]),
            "psign": psign,
            "meta": np.array([[t0, 1 - m_, m_ * 8, NT - t0, (1 - m_) * 8, 0, 0, 0]],
                             np.int32),
            "flags": np.array([[float(m_), 0, 0, 0, 0, 0, 0, 0]], f32),
        }
        ins.append(im)
    return ins


def kernel(**inputs):
    ins = _prep_inputs(inputs)
    nc, st = build()
    res = run_bass_kernel_spmd(nc, ins, core_ids=list(range(8)))
    st.close()
    outs = np.empty((B, T, DIM), np.float32)
    for core in range(8):
        b_, m_ = core // 2, core % 2
        outs[b_, NT * m_: NT * (m_ + 1), :] = res.results[core]["res"].T
    return outs
